# version 15
# baseline (speedup 1.0000x reference)
import numpy as np
import ml_dtypes
import concourse.bass as bass
import concourse.mybir as mybir
from concourse.bass_utils import run_bass_kernel_spmd
from contextlib import ExitStack

F32 = mybir.dt.float32
BF16 = mybir.dt.bfloat16
I32 = mybir.dt.int32
AF = mybir.ActivationFunctionType
ALU = mybir.AluOpType
AX = mybir.AxisListType

NEG = -1.0e30


class Cfg:
    def __init__(self, KT=16, NT=16, E=8, NIT=11, NSH=4, H=16, KVH=4):
        self.KT, self.NT, self.E, self.NIT, self.NSH = KT, NT, E, NIT, NSH
        self.H, self.KVH = H, KVH
        self.D = KT * 128
        self.TOK = NT * 128
        self.S = 2 * self.TOK
        self.SKT = self.S // 128
        self.NSEG = max(1, self.TOK // 512)
        self.SEG = min(512, self.TOK)
        self.NG = NT // 4
        self.I = NIT * 128
        self.SI = NSH * NIT * 128
        self.VW = KVH * 128
        self.SC = 128 ** -0.5
        assert NT % 4 == 0 and self.TOK % 512 == 0 or NT == 4


FULL = Cfg()


class Sm:
    __slots__ = ("h", "n", "name")

    def __init__(self, h, name):
        self.h, self.n, self.name = h, 0, name


class Prog:
    ENGS = ("sync", "gpsimd", "tensor", "vector", "scalar")

    def __init__(self):
        self.q = {e: [] for e in self.ENGS}
        self.lastw = {e: {} for e in self.ENGS}

    def w(self, eng, sem, val):
        val = int(val)
        if val <= 0:
            return
        if self.lastw[eng].get(sem.name, -1) >= val:
            return
        self.lastw[eng][sem.name] = val
        self.q[eng].append(("w", sem, val))

    def do(self, eng, fn, incs=()):
        for s, k in incs:
            s.n += k
        self.q[eng].append(("c", fn, tuple(incs)))

    def raw(self, eng, fn):
        self.q[eng].append(("r", fn))

    def replay(self, eng, obj):
        for item in self.q[eng]:
            if item[0] == "w":
                obj.wait_ge(item[1].h, item[2])
            elif item[0] == "c":
                ins = item[1](obj)
                for s, k in item[2]:
                    ins.then_inc(s.h, k)
            else:
                item[1](obj)


def build(cfg, dbg=False):
    nc = bass.Bass("TRN2", target_bir_lowering=False)
    vh_cell = {}
    c = cfg
    D, TOK, NT, KT, H, KVH = c.D, c.TOK, c.NT, c.KT, c.H, c.KVH
    TOKB, DB = TOK * 2, D * 2
    NM = D // 512
    NCH = TOK // 512 if TOK >= 512 else 1

    x0 = nc.dram_tensor("x0", [TOK, D], F32, kind="ExternalInput").ap()
    wq_d = nc.dram_tensor("wq", [D, H * 128], F32, kind="ExternalInput").ap()
    wk_d = nc.dram_tensor("wk", [D, KVH * 128], F32, kind="ExternalInput").ap()
    wv_d = nc.dram_tensor("wv", [D, KVH * 128], F32, kind="ExternalInput").ap()
    wo_d = nc.dram_tensor("wo", [H * 128, D], F32, kind="ExternalInput").ap()
    gq_d = nc.dram_tensor("gq", [1, 128], F32, kind="ExternalInput").ap()
    gk_d = nc.dram_tensor("gk", [1, 128], F32, kind="ExternalInput").ap()
    r9_d = nc.dram_tensor("r9w", [D, c.E + 1], F32, kind="ExternalInput").ap()
    wg_d = nc.dram_tensor("wgate", [c.E, D, c.I], F32, kind="ExternalInput").ap()
    wu_d = nc.dram_tensor("wup", [c.E, D, c.I], F32, kind="ExternalInput").ap()
    wd_d = nc.dram_tensor("wdown", [c.E, c.I, D], F32, kind="ExternalInput").ap()
    swg_d = nc.dram_tensor("swg", [D, c.SI], F32, kind="ExternalInput").ap()
    swu_d = nc.dram_tensor("swu", [D, c.SI], F32, kind="ExternalInput").ap()
    swd_d = nc.dram_tensor("swd", [c.SI, D], F32, kind="ExternalInput").ap()
    mp1_d = nc.dram_tensor("mp1", [NT, 128, 512], BF16, kind="ExternalInput").ap()
    mp2_d = nc.dram_tensor("mp2", [c.NG, 8, 128, 512], BF16, kind="ExternalInput").ap()
    hsel_d = nc.dram_tensor("hsel", [1, 64], I32, kind="ExternalInput").ap()
    nonce_d = nc.dram_tensor("noncein", [1, 64], I32, kind="ExternalInput").ap()
    idb_d = nc.dram_tensor("identb", [128, 128], BF16, kind="ExternalInput").ap()
    id32_d = nc.dram_tensor("ident32", [128, 128], F32, kind="ExternalInput").ap()

    out_d = nc.dram_tensor("out", [TOK, D], F32, kind="ExternalOutput").ap()

    x1d = nc.dram_tensor("x1d", [TOK, D], F32, kind="Internal").ap()
    x2d = nc.dram_tensor("x2d", [TOK, D], F32, kind="Internal").ap()
    accd = nc.dram_tensor("accd", [TOK, D], F32, kind="Internal").ap()
    attnTd = nc.dram_tensor("attnTd", [H, 128, TOK], BF16, kind="Internal").ap()
    kvh_elems = KVH * 128 * TOK + NT * 128 * c.VW
    kv_mine = nc.dram_tensor("kv_mine", [kvh_elems], BF16, kind="Internal").ap()
    sh_kv = nc.dram_tensor("sh_kv", [2, kvh_elems], BF16, kind="Internal",
                           addr_space="Shared").ap()
    sh_flag = nc.dram_tensor("sh_flag", [2, 64], I32, kind="Internal",
                             addr_space="Shared").ap()
    dbg_t = {}
    if dbg:
        for nm, shape, dt in [("dacc1", [TOK, D], F32), ("dx1", [TOK, D], F32),
                              ("dattnT", [H, 128, TOK], BF16),
                              ("dx2", [TOK, D], F32),
                              ("dkv", [2, kvh_elems], BF16)]:
            dbg_t[nm] = nc.dram_tensor(nm, shape, dt, kind="ExternalOutput").ap()

    P = Prog()
    stack = ExitStack()
    ent = stack.enter_context

    s_x = Sm(ent(nc.semaphore("s_x")), "s_x")
    s_st = Sm(ent(nc.semaphore("s_st")), "s_st")
    s_wg = Sm(ent(nc.semaphore("s_wg")), "s_wg")
    s_acc = Sm(ent(nc.semaphore("s_acc")), "s_acc")
    s_kvw = Sm(ent(nc.semaphore("s_kvw")), "s_kvw")
    s_bar = Sm(ent(nc.semaphore("s_bar")), "s_bar")
    s_pe = Sm(ent(nc.semaphore("s_pe")), "s_pe")
    s_dv = Sm(ent(nc.semaphore("s_dv")), "s_dv")
    s_sc = Sm(ent(nc.semaphore("s_sc")), "s_sc")
    s_gb = Sm(ent(nc.semaphore("s_gb")), "s_gb")
    s_rr = Sm(ent(nc.semaphore("s_rr")), "s_rr")

    identb = ent(nc.sbuf_tensor("sb_identb", [128, 128], BF16))
    ident32 = ent(nc.sbuf_tensor("sb_ident32", [128, 128], F32))
    ones128b = ent(nc.sbuf_tensor("ones128b", [128, 128], BF16))
    onesrow32 = ent(nc.sbuf_tensor("onesrow32", [1, 128], F32))
    grow = ent(nc.sbuf_tensor("grow", [1, 256], F32))
    gpcol = ent(nc.sbuf_tensor("gpcol", [128, 8], F32))
    r9 = ent(nc.sbuf_tensor("r9", [128, KT * (c.E + 1)], BF16))
    comb = ent(nc.sbuf_tensor("comb", [128, NT * c.E], F32))
    gcol = ent(nc.sbuf_tensor("gcol", [128, NT], F32))
    ab = ent(nc.sbuf_tensor("ab", [128, 2 * NT], F32))
    ivec = ent(nc.sbuf_tensor("ivec", [1, 128], I32))
    fpoll = ent(nc.sbuf_tensor("fpoll", [1, 128], I32))
    stat = ent(nc.sbuf_tensor("stat", [128, 64], F32))
    rrow = ent(nc.sbuf_tensor("rrow", [1, 512], F32))
    rr4 = ent(nc.sbuf_tensor("rr4", [4, 128], F32))
    onesrowb = ent(nc.sbuf_tensor("onesrowb", [1, 128], BF16))
    rhis = [ent(nc.sbuf_tensor("rhi0", [1, 512], BF16)),
            ent(nc.sbuf_tensor("rhi1", [1, 512], BF16))]
    rlos = [ent(nc.sbuf_tensor("rlo0", [1, 512], BF16)),
            ent(nc.sbuf_tensor("rlo1", [1, 512], BF16))]
    tiny = ent(nc.sbuf_tensor("tiny", [128, 32], F32))
    r_h = ent(nc.gpsimd.register("r_h"))

    AR_B = 199680
    arena = ent(nc.sbuf_tensor("arena", [128, AR_B // 2], BF16))

    def A(off, nbytes, dtype=BF16):
        assert off % 64 == 0 and off + nbytes <= AR_B, (off, nbytes)
        ap = arena[:, off // 2:(off + nbytes) // 2]
        return ap if dtype == BF16 else ap.bitcast(dtype)

    psA = ent(nc.psum_tensor("psA", [128, 2048], F32))
    psB = ent(nc.psum_tensor("psB", [128, 2048], F32))

    def pA(c0, n):
        return psA[:, c0:c0 + n]

    def pB(c0, n):
        return psB[:, c0:c0 + n]

    def pAb(c0f32, nb):
        return psA.bitcast(BF16)[:, 2 * c0f32:2 * c0f32 + nb]

    def pBb(c0f32, nb):
        return psB.bitcast(BF16)[:, 2 * c0f32:2 * c0f32 + nb]

    o_xbuf = 0
    o_wgu = 0
    wgu_sz = 2 * KT * 256
    o_ybuf = 2 * D * 4
    o_stage = o_ybuf + 2 * DB
    o_yT = o_stage + 2 * 4096
    o_h = o_yT + KT * TOKB
    o_wd = o_h + c.NIT * TOKB
    ffn_end = o_wd + c.NIT * DB
    assert ffn_end <= AR_B, ffn_end
    assert 2 * wgu_sz <= o_ybuf

    o_y2T = o_yT
    o_QT = o_h
    o_wkv = o_QT + H * TOKB
    wkv_sz = KT * c.VW * 2
    p3_end = o_wkv + 2 * wkv_sz
    assert p3_end <= AR_B, p3_end
    o_wqr = 0
    o_qsq = o_wqr + 2 * KT * 256
    o_rsd = o_qsq + 2048
    o_evk = o_rsd + 2 * 2048
    assert o_evk + 2 * 1024 <= o_yT

    P4S = 8 * 1024 + 2 * 4096 + 4 * 1024 + 4096 + 2 * 1024 + 2048
    if P4S <= min(o_yT, o_QT):
        o_p4s = 0
        o_Kf = o_yT
        kf_lim = o_QT
    else:
        o_Kf = ((o_QT + H * TOKB + 63) // 64) * 64
        o_p4s = o_Kf + KVH * c.S * 2 + c.SKT * c.VW * 2
        kf_lim = AR_B
    o_mp2 = o_p4s
    o_pT = o_mp2 + 8 * 1024
    o_mp1 = o_pT + 2 * 4096
    o_dump = o_mp1 + 4 * 1024
    o_atev = o_dump + 4096
    o_frep = o_atev + 2 * 1024
    o_Vf = o_Kf + KVH * c.S * 2
    p4_end = max(o_Vf + c.SKT * c.VW * 2, o_frep + 2048)
    assert p4_end <= kf_lim or o_p4s > 0, (p4_end, kf_lim)
    assert o_frep + 2048 <= AR_B
    o_sst = ((max(o_QT + H * TOKB, p4_end, o_frep + 2048) + 63) // 64) * 64
    sst_bytes = sum((4 * (c.NG - 1) + s_ + 2) // 2 for s_ in range(4)) * 1024
    assert o_sst + sst_bytes <= AR_B, (o_sst, sst_bytes)

    o_aT = 0
    o_wo = o_aT + H * TOKB
    o_x1b = o_wo + H * DB
    o_u1 = o_x1b + 2 * D * 4
    o_x2b = o_u1 + D * 4
    assert o_x2b + 2 * D * 4 <= AR_B

    o_7a = 0
    o_7b = 2 * D * 4
    o_7c = 4 * D * 4

    def fence():
        for eng in Prog.ENGS:
            for sm in (s_pe, s_dv, s_sc, s_x, s_st, s_wg, s_acc, s_kvw):
                P.w(eng, sm, sm.n)

    def p0():
        P.do("sync", lambda e: e.dma_start(out=identb[:], in_=idb_d), [(s_x, 16)])
        P.do("sync", lambda e: e.dma_start(out=ident32[:], in_=id32_d), [(s_x, 16)])
        P.do("sync", lambda e: e.dma_start(out=grow[0:1, 0:128], in_=gq_d), [(s_x, 16)])
        P.do("sync", lambda e: e.dma_start(out=grow[0:1, 128:256], in_=gk_d), [(s_x, 16)])
        P.do("gpsimd", lambda e: e.dma_start(out=ivec[0:1, 0:64], in_=hsel_d), [(s_wg, 16)])
        P.do("gpsimd", lambda e: e.dma_start(out=ivec[0:1, 64:128], in_=nonce_d), [(s_wg, 16)])
        P.do("gpsimd", lambda e: e.dma_start(
            out=r9[:], in_=r9_d.rearrange("(kt p) c -> p kt c", p=128)), [(s_wg, 16)])
        P.w("gpsimd", s_wg, 48)

        def _loadh(e):
            e.reg_load(r_h, ivec[0:1, 0:1])
            vh_cell["v"] = e.snap(r_h, min_val=0, max_val=1)
            return e.nop()
        P.do("gpsimd", _loadh, [(s_gb, 1)])
        P.w("vector", s_x, 64)
        P.do("vector", lambda e: e.memset(ones128b[:], 1.0), ())
        P.do("vector", lambda e: e.memset(onesrow32[:], 1.0), ())
        P.do("vector", lambda e: e.memset(onesrowb[:], 1.0), ())
        P.do("vector", lambda e: e.memset(stat[:, 60:61], 1e-5), ())
        P.do("vector", lambda e: e.memset(stat[:, 61:62], 1e-8), [(s_dv, 1)])
        P.do("vector", lambda e: e.tensor_tensor(
            grow[0:1, 0:128], grow[0:1, 0:128], grow[0:1, 128:256], op=ALU.mult),
            [(s_dv, 1)])
        P.w("tensor", s_dv, s_dv.n)
        P.do("tensor", lambda e: e.matmul(
            pA(0, 1), grow[0:1, 0:128], onesrow32[0:1, 0:1],
            start=True, stop=True), [(s_pe, 1)])
        P.w("scalar", s_pe, s_pe.n)
        P.do("scalar", lambda e: e.copy(gpcol[:, 0:1], pA(0, 1)),
             [(s_sc, 1)])
        return s_sc.n, s_dv.n

    def ln_tile(t, xb, ybf, sq, yT_off, extra_wait_pe=0):
        P.do("vector", lambda e: e.reduce_sum(tiny[:, 0:1], xb[:], axis=AX.X),
             [(s_dv, 1)])
        P.w("scalar", s_dv, s_dv.n)
        P.do("scalar", lambda e: e.mul(tiny[:, 1:2], tiny[:, 0:1], 1.0 / D),
             [(s_sc, 1)])
        P.w("vector", s_sc, s_sc.n)
        P.do("vector", lambda e: e.tensor_scalar(
            xb[:], xb[:], tiny[:, 1:2], None, op0=ALU.subtract), [(s_dv, 1)])
        P.w("scalar", s_dv, s_dv.n)
        P.do("scalar", lambda e: e.activation(
            sq[:], xb[:], AF.Square, accum_out=tiny[:, 2:3]), [(s_sc, 1)])
        P.w("scalar", s_sc, s_sc.n)
        P.do("scalar", lambda e: e.activation(
            tiny[:, 3:4], tiny[:, 2:3], AF.Ln, scale=1.0 / D,
            bias=stat[:, 60:61]), [(s_sc, 1)])
        P.w("scalar", s_sc, s_sc.n)
        P.do("scalar", lambda e: e.activation(
            tiny[:, 3:4], tiny[:, 3:4], AF.Exp, scale=-0.5), [(s_sc, 1)])
        P.w("scalar", s_sc, s_sc.n)
        if extra_wait_pe:
            P.w("scalar", s_pe, extra_wait_pe)
        P.do("scalar", lambda e: e.activation(
            ybf[:], xb[:], AF.Copy, scale=tiny[:, 3:4]), [(s_sc, 1)])
        y_sc = s_sc.n
        evs = []
        for k in range(KT):
            P.w("tensor", s_sc, y_sc)
            slot = pAb(0, 128) if k % 2 == 0 else pBb(0, 128)
            if len(ln_tr_ev) >= 2:
                P.w("tensor", s_dv, ln_tr_ev[-2])
            P.do("tensor", lambda e, k=k, slot=slot: e.transpose(
                slot, ybf[:, k * 128:(k + 1) * 128], identb[:]), [(s_pe, 1)])
            P.w("vector", s_pe, s_pe.n)
            P.do("vector", lambda e, k=k, slot=slot: e.tensor_copy(
                A(yT_off + k * TOKB, TOKB)[:, t * 128:(t + 1) * 128], slot),
                [(s_dv, 1)])
            ln_tr_ev.append(s_dv.n)
            evs.append(s_dv.n)
        return y_sc, evs, s_pe.n

    ln_tr_ev = []

    def ffn_f0(src, src_ready_st):
        xb = [A(o_xbuf, D * 4, F32), A(o_xbuf + D * 4, D * 4, F32)]
        yb = [A(o_ybuf, DB), A(o_ybuf + DB, DB)]
        sq = A(o_stage, D * 4, F32)
        y_done, tr_pe, comb_done, gate_sc = [], [], [], []
        ld = []

        def f0_load(t):
            if src_ready_st:
                P.w("sync", s_st, src_ready_st)
            if t >= 2:
                P.w("sync", s_sc, y_done[t - 2])
            P.do("sync", lambda e, t=t: e.dma_start(
                out=xb[t % 2][:], in_=src[t * 128:(t + 1) * 128, :]), [(s_x, 16)])
            ld.append(s_x.n)

        f0_load(0)
        for t in range(NT):
            if t + 1 < NT:
                f0_load(t + 1)
            P.w("vector", s_x, ld[-1])
            ysc, evs, pen = ln_tile(t, xb[t % 2], yb[t % 2], sq, o_yT,
                                    extra_wait_pe=(tr_pe[t - 2] if t >= 2 else 0))
            y_done.append(ysc)
            tr_pe.append(pen)
            ne1 = c.E + 1
            psR = pA(512, ne1)
            if t >= 1:
                P.w("tensor", s_dv, comb_done[t - 1])
                P.w("tensor", s_sc, gate_sc[t - 1])
            for k in range(KT):
                P.w("tensor", s_dv, evs[k])
                P.do("tensor", lambda e, k=k, t=t: e.matmul(
                    psR, A(o_yT + k * TOKB, TOKB)[:, t * 128:(t + 1) * 128],
                    r9[:, k * ne1:(k + 1) * ne1], start=(k == 0),
                    stop=(k == KT - 1)),
                    [(s_pe, 1)] if k == KT - 1 else ())
            P.w("vector", s_pe, s_pe.n)
            L = psR[:, 0:c.E]

            def ve(fn):
                P.do("vector", fn, [(s_dv, 1)])
                P.w("vector", s_dv, s_dv.n)

            ve(lambda e: e.reduce_max(tiny[:, 4:5], L, axis=AX.X, negate=True))
            ve(lambda e: e.tensor_scalar(
                stat[:, 0:c.E], L, tiny[:, 4:5], None, op0=ALU.add))
            P.w("scalar", s_dv, s_dv.n)
            P.do("scalar", lambda e: e.activation(
                stat[:, 8:8 + c.E], stat[:, 0:c.E], AF.Exp), [(s_sc, 1)])
            ve(lambda e: e.tensor_scalar(
                stat[:, 16:16 + c.E], stat[:, 0:c.E], 0.0, None, op0=ALU.is_ge))
            ve(lambda e: e.scalar_tensor_tensor(
                stat[:, 16:16 + c.E], stat[:, 16:16 + c.E], NEG, stat[:, 0:c.E],
                op0=ALU.mult, op1=ALU.add))
            ve(lambda e: e.reduce_max(
                tiny[:, 5:6], stat[:, 16:16 + c.E], axis=AX.X, negate=True))
            ve(lambda e: e.tensor_scalar(
                stat[:, 16:16 + c.E], stat[:, 0:c.E], tiny[:, 5:6], 0.0,
                op0=ALU.add, op1=ALU.is_ge))
            P.w("vector", s_sc, s_sc.n)
            ve(lambda e: e.tensor_tensor(
                stat[:, 16:16 + c.E], stat[:, 16:16 + c.E], stat[:, 8:8 + c.E],
                op=ALU.mult))
            ve(lambda e: e.reduce_sum(
                tiny[:, 6:7], stat[:, 16:16 + c.E], axis=AX.X))
            ve(lambda e: e.reciprocal(tiny[:, 7:8], tiny[:, 6:7]))
            P.w("scalar", s_dv, s_dv.n)
            P.do("scalar", lambda e: e.mul(tiny[:, 7:8], tiny[:, 7:8], 0.5),
                 [(s_sc, 1)])
            P.w("vector", s_sc, s_sc.n)
            P.do("vector", lambda e, t=t: e.tensor_scalar(
                comb[:, t * c.E:(t + 1) * c.E], stat[:, 16:16 + c.E],
                tiny[:, 7:8], None, op0=ALU.mult), [(s_dv, 1)])
            comb_done.append(s_dv.n)
            P.do("scalar", lambda e: e.activation(
                tiny[:, 9:10], psR[:, c.E:c.E + 1], AF.Exp, scale=-1.0),
                [(s_sc, 1)])
            P.w("scalar", s_sc, s_sc.n)
            P.do("scalar", lambda e: e.add(tiny[:, 9:10], tiny[:, 9:10], 1.0),
                 [(s_sc, 1)])
            gate_sc.append(s_sc.n)
            P.w("vector", s_sc, s_sc.n)
            ve(lambda e: e.reciprocal(tiny[:, 10:11], tiny[:, 9:10]))
            P.do("vector", lambda e, t=t: e.tensor_scalar(
                gcol[:, t:t + 1], tiny[:, 10:11], 0.5, None, op0=ALU.mult),
                [(s_dv, 1)])
            comb_done[-1] = s_dv.n
        return s_dv.n, s_sc.n, s_pe.n

    def ffn_f1(f0_dv, f0_sc):
        units = []
        for e_ in range(c.E):
            units.append((wg_d[e_], wu_d[e_], wd_d[e_],
                          lambda t, e_=e_: comb[:, t * c.E + e_:t * c.E + e_ + 1]))
        for cc in range(c.NSH):
            i0 = cc * c.I
            units.append((swg_d[:, i0:i0 + c.I], swu_d[:, i0:i0 + c.I],
                          swd_d[i0:i0 + c.I, :], lambda t: gcol[:, t:t + 1]))
        wgu = [A(o_wgu, wgu_sz), A(o_wgu + wgu_sz, wgu_sz)]
        stg = [A(o_stage, 4096, F32), A(o_stage + 4096, 4096, F32)]

        gu_need, gu_done = {}, {}
        evac_keys = []
        evac_dv = {}
        down_ev = []
        stage_acc = []
        seq = 0

        def emit_wgu_load(u, iT, wgs, wus):
            g = seq + iT
            if g >= 2:
                P.w("gpsimd", s_pe, gu_done[g - 2])
            else:
                P.w("gpsimd", s_sc, f0_sc)
            ic = slice(iT * 128, (iT + 1) * 128)
            P.do("gpsimd", lambda e, wgs=wgs, ic=ic, g=g: e.dma_start(
                out=wgu[g % 2][:, 0:KT * 128],
                in_=wgs[:, ic].rearrange("(kt p) i -> p kt i", p=128)),
                [(s_wg, 16)])
            P.do("gpsimd", lambda e, wus=wus, ic=ic, g=g: e.dma_start(
                out=wgu[g % 2][:, KT * 128:2 * KT * 128],
                in_=wus[:, ic].rearrange("(kt p) i -> p kt i", p=128)),
                [(s_wg, 16)])
            gu_need[g] = s_wg.n

        for u, (wgs, wus, wds, scf) in enumerate(units):
            for iT in range(min(2, c.NIT)):
                emit_wgu_load(u, iT, wgs, wus)
            if u > 0:
                P.w("gpsimd", s_pe, down_done)
            for iT in range(c.NIT):
                P.do("gpsimd", lambda e, wds=wds, iT=iT: e.dma_start(
                    out=A(o_wd + iT * DB, DB),
                    in_=wds[iT * 128:(iT + 1) * 128, :]), [(s_wg, 16)])
            wd_need = s_wg.n

            if u == 0:
                P.w("tensor", s_dv, f0_dv)
                P.w("tensor", s_sc, f0_sc)
            else:
                P.w("tensor", s_dv, down_ev[-1])
            for iT in range(c.NIT):
                g = seq + iT
                P.w("tensor", s_wg, gu_need[max(g, min(seq + 1, seq + c.NIT - 1))]
                    if iT < 2 else gu_need[g])
                sg = 0
                while sg < c.NSEG:
                    npair = min(2, c.NSEG - sg)
                    par = len(evac_keys) % 2
                    gbanks = [pA((par * 2 + i) % 4 * 512, c.SEG)
                              for i in range(npair)]
                    ubanks = [pB((par * 2 + i) % 4 * 512, c.SEG)
                              for i in range(npair)]
                    ek = (u, iT, sg)
                    if len(evac_keys) >= 2:
                        P.w("tensor", s_dv, evac_dv[evac_keys[-2]])
                    tcs = [slice((sg + i) * c.SEG, (sg + i + 1) * c.SEG)
                           for i in range(npair)]
                    for k in range(KT):
                        for i in range(npair):
                            P.do("tensor", lambda e, k=k, g=g, b=gbanks[i],
                                 tc=tcs[i]: e.matmul(
                                     b, wgu[g % 2][:, k * 128:(k + 1) * 128],
                                     A(o_yT + k * TOKB, TOKB)[:, tc],
                                     start=(k == 0), stop=(k == KT - 1)), ())
                        for i in range(npair):
                            P.do("tensor", lambda e, k=k, g=g, b=ubanks[i],
                                 tc=tcs[i]: e.matmul(
                                     b, wgu[g % 2][:, KT * 128 + k * 128:
                                                   KT * 128 + (k + 1) * 128],
                                     A(o_yT + k * TOKB, TOKB)[:, tc],
                                     start=(k == 0), stop=(k == KT - 1)),
                                 [(s_pe, 1)] if (k == KT - 1 and
                                                 i == npair - 1) else ())
                    P.w("scalar", s_pe, s_pe.n)
                    for i in range(npair):
                        P.do("scalar", lambda e, iT=iT, b=gbanks[i], tc=tcs[i]:
                             e.activation(A(o_h + iT * TOKB, TOKB)[:, tc], b,
                                          AF.Silu),
                             [(s_sc, 1)] if i == npair - 1 else ())
                    P.w("vector", s_sc, s_sc.n)
                    for i in range(npair):
                        P.do("vector", lambda e, iT=iT, b=ubanks[i], tc=tcs[i]:
                             e.tensor_tensor(A(o_h + iT * TOKB, TOKB)[:, tc],
                                             A(o_h + iT * TOKB, TOKB)[:, tc],
                                             b, op=ALU.mult),
                             [(s_dv, 1)] if i == npair - 1 else ())
                    evac_keys.append(ek)
                    evac_dv[ek] = s_dv.n
                    sg += npair
                gu_done[g] = s_pe.n
                if iT + 2 < c.NIT:
                    emit_wgu_load(u, iT + 2, wgs, wus)
            seq += c.NIT
            gu_all = s_dv.n

            P.w("tensor", s_wg, wd_need)
            P.w("tensor", s_dv, gu_all)
            for t in range(NT):
                ps = psA if t % 2 == 0 else psB
                if len(down_ev) >= 2:
                    P.w("tensor", s_dv, down_ev[-2])
                for iT in range(c.NIT):
                    for m in range(NM):
                        P.do("tensor", lambda e, t=t, iT=iT, m=m, ps=ps: e.matmul(
                            ps[:, m * 512:(m + 1) * 512],
                            A(o_h + iT * TOKB, TOKB)[:, t * 128:(t + 1) * 128],
                            A(o_wd + iT * DB, DB)[:, m * 512:(m + 1) * 512],
                            start=(iT == 0), stop=(iT == c.NIT - 1)),
                            [(s_pe, 1)] if (iT == c.NIT - 1 and m == NM - 1)
                            else ())
                dmm = s_pe.n
                npc = max(1, D // 1024)
                cw = min(1024, D)
                for mh in range(npc):
                    slot = len(stage_acc) % 2
                    if len(stage_acc) >= 2:
                        P.w("vector", s_acc, stage_acc[-2])
                    P.w("vector", s_pe, dmm)
                    P.do("vector", lambda e, t=t, mh=mh, ps=ps, slot=slot, cw=cw,
                         scf=scf: e.tensor_scalar(
                             stg[slot][:, 0:cw], ps[:, mh * cw:(mh + 1) * cw],
                             scf(t), None, op0=ALU.mult), [(s_dv, 1)])
                    P.w("gpsimd", s_dv, s_dv.n)
                    kw = {} if u == 0 else {"accum_op": ALU.add}
                    P.do("gpsimd", lambda e, t=t, mh=mh, slot=slot, cw=cw, kw=kw:
                         e.dma_start(out=accd[t * 128:(t + 1) * 128,
                                              mh * cw:(mh + 1) * cw],
                                     in_=stg[slot][:, 0:cw], **kw), [(s_acc, 16)])
                    stage_acc.append(s_acc.n)
                down_ev.append(s_dv.n)
            down_done = s_pe.n
        return s_acc.n

    def p2(acc_total):
        xb = [A(o_xbuf, D * 4, F32), A(o_xbuf + D * 4, D * 4, F32)]
        ac = [A(o_h, D * 4, F32), A(o_h + D * 4, D * 4, F32)]
        yb = [A(o_ybuf, DB), A(o_ybuf + DB, DB)]
        sq = A(o_stage, D * 4, F32)
        P.w("sync", s_acc, acc_total)
        y_done, st_list, tr_pe = [], [], []
        ld = []

        def p2_load(t):
            if t >= 2:
                P.w("sync", s_sc, y_done[t - 2])
            P.do("sync", lambda e, t=t: e.dma_start(
                out=xb[t % 2][:], in_=x0[t * 128:(t + 1) * 128, :]), [(s_x, 16)])
            P.do("sync", lambda e, t=t: e.dma_start(
                out=ac[t % 2][:], in_=accd[t * 128:(t + 1) * 128, :]), [(s_x, 16)])
            ld.append(s_x.n)

        p2_load(0)
        for t in range(NT):
            if t + 1 < NT:
                p2_load(t + 1)
            P.w("vector", s_x, ld[-1])
            P.do("vector", lambda e, t=t: e.tensor_tensor(
                xb[t % 2][:], xb[t % 2][:], ac[t % 2][:], op=ALU.add), [(s_dv, 1)])
            P.w("sync", s_dv, s_dv.n)
            P.do("sync", lambda e, t=t: e.dma_start(
                out=x1d[t * 128:(t + 1) * 128, :], in_=xb[t % 2][:]), [(s_st, 16)])
            st_list.append(s_st.n)
            P.w("vector", s_st, s_st.n)
            ysc, evs, pen = ln_tile(t, xb[t % 2], yb[t % 2], sq, o_y2T,
                                    extra_wait_pe=(tr_pe[t - 2] if t >= 2 else 0))
            y_done.append(ysc)
            tr_pe.append(pen)
            P.do("scalar", lambda e, t=t: e.add(
                ab[:, 2 * t:2 * t + 1], tiny[:, 3:4], 1.0), [(s_sc, 1)])
            P.w("vector", s_sc, s_sc.n)
            P.do("vector", lambda e, t=t: e.tensor_scalar(
                ab[:, 2 * t + 1:2 * t + 2], tiny[:, 1:2], tiny[:, 3:4], -1.0,
                op0=ALU.mult, op1=ALU.mult), [(s_dv, 1)])
        return s_dv.n, s_st.n

    def p3(p2_dv):
        wkv = A(o_wkv, wkv_sz)
        wvv = A(o_wkv + wkv_sz, wkv_sz)
        wqr = [A(o_wqr, KT * 256), A(o_wqr + KT * 256, KT * 256)]
        qsq = A(o_qsq, 2048)
        rsd = [A(o_rsd, 2048, F32), A(o_rsd + 2048, 2048, F32)]
        evk = [A(o_evk, 1024), A(o_evk + 1024, 1024)]

        cwT = min(512, TOK)

        def kvm_K(kv, ch):
            base = kv * 128 * TOK
            return kv_mine[base:base + 128 * TOK].rearrange(
                "(p chh f) -> chh p f", p=128, chh=NCH, f=cwT)[ch]

        def kvm_V(t):
            base = KVH * 128 * TOK + t * 128 * c.VW
            return kv_mine[base:base + 128 * c.VW].rearrange(
                "(p f) -> p f", p=128)
        P.do("gpsimd", lambda e: e.dma_start(
            out=wkv, in_=wk_d.rearrange("(kt p) i -> p kt i", p=128)), [(s_wg, 16)])
        P.do("gpsimd", lambda e: e.dma_start(
            out=wvv, in_=wv_d.rearrange("(kt p) i -> p kt i", p=128)), [(s_wg, 16)])
        kvw_need = s_wg.n

        def rms_chunk(ch, bank, gamma):
            P.w("scalar", s_dv, s_dv.n)
            P.w("scalar", s_pe, s_pe.n)
            P.do("scalar", lambda e: e.activation(qsq[:, 0:512], bank, AF.Square),
                 [(s_sc, 1)])
            msb = pB(1024, 512)
            P.w("tensor", s_sc, s_sc.n)
            P.do("tensor", lambda e, msb=msb: e.matmul(
                msb, ones128b[:], qsq[:, 0:512], start=True, stop=True),
                [(s_pe, 1)])
            P.w("scalar", s_pe, s_pe.n)
            P.do("scalar", lambda e, ch=ch, msb=msb: e.activation(
                rsd[ch % 2][:, 0:512], msb, AF.Ln, scale=1.0 / 128,
                bias=stat[:, 61:62]), [(s_sc, 1)])
            P.w("scalar", s_sc, s_sc.n)
            P.do("scalar", lambda e, ch=ch: e.activation(
                rsd[ch % 2][:, 0:512], rsd[ch % 2][:, 0:512], AF.Exp, scale=-0.5),
                [(s_sc, 1)])

        P.w("tensor", s_dv, p2_dv)
        P.w("tensor", s_wg, kvw_need)
        kev, kst = [], []
        for kv in range(KVH):
            for ch in range(NCH):
                bank = pA((len(kev) % 2) * 512, min(512, TOK))
                if len(kev) >= 2:
                    P.w("tensor", s_dv, kev[-2])
                cw = min(512, TOK)
                for k in range(KT):
                    P.do("tensor", lambda e, k=k, kv=kv, ch=ch, bank=bank, cw=cw:
                         e.matmul(bank,
                                  wkv[:, (k * KVH + kv) * 128:(k * KVH + kv + 1) * 128],
                                  A(o_y2T + k * TOKB, TOKB)[:, ch * 512:ch * 512 + cw],
                                  start=(k == 0), stop=(k == KT - 1)),
                         [(s_pe, 1)] if k == KT - 1 else ())
                rms_chunk(ch, bank, True)
                slot = len(kst) % 2
                P.w("vector", s_sc, s_sc.n)
                if len(kst) >= 2:
                    P.w("vector", s_st, kst[-2])
                P.do("vector", lambda e, ch=ch, bank=bank, slot=slot:
                     e.scalar_tensor_tensor(evk[slot][:, 0:512], bank,
                                            gpcol[:, 0:1], rsd[ch % 2][:, 0:512],
                                            op0=ALU.mult, op1=ALU.mult),
                     [(s_dv, 1)])
                kev.append(s_dv.n)
                P.w("sync", s_dv, s_dv.n)
                P.do("sync", lambda e, kv=kv, ch=ch, slot=slot, cw=cw: e.dma_start(
                    out=kvm_K(kv, ch), in_=evk[slot][:, 0:cw]), [(s_st, 16)])
                kst.append(s_st.n)
        vev = []
        voff = KVH * 128 * TOK
        for t in range(NT):
            bank = pA(1024 + (t % 2) * 512, c.VW)
            if len(vev) >= 2:
                P.w("tensor", s_dv, vev[-2])
            for k in range(KT):
                P.do("tensor", lambda e, k=k, t=t, bank=bank: e.matmul(
                    bank, A(o_y2T + k * TOKB, TOKB)[:, t * 128:(t + 1) * 128],
                    wvv[:, k * c.VW:(k + 1) * c.VW],
                    start=(k == 0), stop=(k == KT - 1)),
                    [(s_pe, 1)] if k == KT - 1 else ())
            P.w("vector", s_pe, s_pe.n)
            slot = len(kst) % 2
            if len(kst) >= 2:
                P.w("vector", s_st, kst[-2])
            P.do("vector", lambda e, bank=bank, slot=slot: e.tensor_copy(
                evk[slot][:, 0:c.VW], bank), [(s_dv, 1)])
            vev.append(s_dv.n)
            P.w("sync", s_dv, s_dv.n)
            P.do("sync", lambda e, t=t, slot=slot: e.dma_start(
                out=kvm_V(t), in_=evk[slot][:, 0:c.VW]), [(s_st, 16)])
            kst.append(s_st.n)
        P.w("gpsimd", s_st, kst[-1])
        P.do("gpsimd", lambda e: e.dma_start(
            out=sh_kv[vh_cell["v"]], in_=kv_mine), [(s_kvw, 16)])
        P.w("gpsimd", s_kvw, s_kvw.n)
        fl3 = sh_flag.rearrange("h (o f) -> h o f", o=1)
        P.do("gpsimd", lambda e: e.dma_start(
            out=fl3[vh_cell["v"]], in_=ivec[0:1, 64:128]),
            [(s_kvw, 16)])

        q_need, qh_done, qev = {}, [], []

        def emit_wq_load(hd):
            if hd >= 2:
                P.w("gpsimd", s_pe, qh_done[hd - 2])
            P.do("gpsimd", lambda e, hd=hd: e.dma_start(
                out=wqr[hd % 2][:],
                in_=wq_d[:, hd * 128:(hd + 1) * 128].rearrange(
                    "(kt p) i -> p kt i", p=128)), [(s_wg, 16)])
            q_need[hd] = s_wg.n

        for hd in range(min(2, H)):
            emit_wq_load(hd)
        for hd in range(H):
            P.w("tensor", s_wg, q_need[min(hd + 1, 1) if hd < 2 else hd]
                if hd < 2 else q_need[hd])
            for ch in range(NCH):
                cw = min(512, TOK)
                bank = pA((len(qev) % 2) * 512, cw)
                if len(qev) >= 2:
                    P.w("tensor", s_dv, qev[-2])
                for k in range(KT):
                    P.do("tensor", lambda e, k=k, hd=hd, ch=ch, bank=bank, cw=cw:
                         e.matmul(bank, wqr[hd % 2][:, k * 128:(k + 1) * 128],
                                  A(o_y2T + k * TOKB, TOKB)[:,
                                    ch * 512:ch * 512 + cw],
                                  start=(k == 0), stop=(k == KT - 1)),
                         [(s_pe, 1)] if k == KT - 1 else ())
                rms_chunk(ch, bank, False)
                P.w("vector", s_sc, s_sc.n)
                P.do("vector", lambda e, hd=hd, ch=ch, bank=bank, cw=cw:
                     e.tensor_tensor(
                         A(o_QT + hd * TOKB, TOKB)[:, ch * 512:ch * 512 + cw],
                         bank, rsd[ch % 2][:, 0:cw], op=ALU.mult), [(s_dv, 1)])
                qev.append(s_dv.n)
            qh_done.append(s_pe.n)
            if hd + 2 < H:
                emit_wq_load(hd + 2)
        qt_dv = s_dv.n

        def barrier(e):
            with (e.register("r_k") as r_k, e.register("r_f0") as r_f0,
                  e.register("r_f1") as r_f1, e.register("r_nb") as r_nb):
                e.reg_load(r_nb, ivec[0:1, 64:65])
                e.reg_mov(r_k, 0)
                lbl, lbl_end = "kvpoll", "kvpoll_end"
                nc.br(lbl, engines=[e.engine])
                with nc.body(lbl, valid_engines=[e.engine]):
                    e.dma_start(out=fpoll[:],
                                in_=sh_flag.rearrange("a b -> (a b)").unsqueeze(0)
                                ).then_inc(s_bar.h, 16)
                    e.reg_add(r_k, r_k, 16)
                    e.wait_ge(s_bar.h, r_k)
                    e.reg_load(r_f0, fpoll[0:1, 0:1])
                    e.reg_load(r_f1, fpoll[0:1, 64:65])
                    e.reg_alu(r_f0, r_f0, r_nb, op=ALU.bitwise_xor)
                    e.reg_alu(r_f1, r_f1, r_nb, op=ALU.bitwise_xor)
                    e.reg_alu(r_f0, r_f0, r_f1, op=ALU.bitwise_or)
                    nc.br_ne(r_f0, 0, on_true=lbl, on_false=lbl_end,
                             engines=[e.engine])
                nc.switch_bb(lbl_end)
            return e.nop()
        P.raw("gpsimd", barrier)
        P.do("gpsimd", lambda e: e.nop(), [(s_gb, 1)])
        gb_bar = s_gb.n
        P.w("sync", s_gb, gb_bar)
        kv_ld = [0] * KVH
        def emit_kf(kv):
            for hh in range(2):
                P.do("sync", lambda e, kv=kv, hh=hh: e.dma_start(
                    out=A(o_Kf + kv * c.S * 2, c.S * 2).rearrange(
                        "p (lt two f) -> p lt two f", two=2, f=128)[:, :, hh, :],
                    in_=sh_kv[hh][kv * 128 * TOK:(kv + 1) * 128 * TOK].rearrange(
                        "(p lt f) -> p lt f", p=128, f=128)), [(s_x, 16)])
            kv_ld[kv] = s_x.n
        emit_kf(0)
        for hh in range(2):
            P.do("sync", lambda e, hh=hh: e.dma_start(
                out=A(o_Vf, c.SKT * c.VW * 2).rearrange(
                    "p (lt two f) -> p lt two f", two=2, f=c.VW)[:, :, hh, :],
                in_=sh_kv[hh][voff:voff + NT * 128 * c.VW].rearrange(
                    "(t p f) -> p t f", p=128, f=c.VW)), [(s_x, 16)])
        vf_ld = s_x.n
        for kv in range(1, KVH):
            emit_kf(kv)
        kvf_ld = (kv_ld, vf_ld)
        if dbg:
            P.w("sync", s_kvw, s_kvw.n)
            P.w("sync", s_gb, gb_bar)
            P.do("sync", lambda e: e.dma_start(out=dbg_t["dkv"], in_=sh_kv),
                 [(s_st, 16)])
        return qt_dv, kvf_ld

    def p4(qt_dv, kvf_ld):
        Kf = lambda kv: A(o_Kf + kv * c.S * 2, c.S * 2)
        Vf = A(o_Vf, c.SKT * c.VW * 2)
        pT = [A(o_pT, 4096), A(o_pT + 4096, 4096)]
        m1b = [A(o_mp1 + i * 1024, 1024) for i in range(4)]
        dump = A(o_dump, 4096)
        atev = [A(o_atev, 1024), A(o_atev + 1024, 1024)]
        kv_ld, vf_ld = kvf_ld
        P.w("tensor", s_dv, qt_dv)
        exp_done, at_st, av_mm = [], [], []
        last_exp_sc = 0
        last_psA_dv = 0
        last_at_ev = [0]
        rr_cp = []
        rr_dma = []
        frep_sc = []
        bq_dv = {}

        def gen_pass1(a, hd, mld):
            nonlocal last_psA_dv
            kvv = hd // (H // KVH)
            nchs = [(4 * a + s_ + 2) // 2 for s_ in range(4)]
            cum = [sum(nchs[:s_]) for s_ in range(4)]
            sstf = lambda s_: A(o_sst + cum[s_] * 1024, nchs[s_] * 1024)
            P.w("tensor", s_dv, last_psA_dv)
            P.w("tensor", s_x, kv_ld[kvv])
            if rr_cp:
                P.w("tensor", s_dv, rr_cp[-1])
            ring = []
            for s_ in range(4):
                nch = nchs[s_]
                for cc in range(nch):
                    bk = len(ring) % 3
                    if len(ring) >= 3:
                        dvt, sct = ring[-3]
                        P.w("tensor", s_dv, dvt)
                        P.w("tensor", s_sc, sct)
                    P.do("tensor", lambda e, hd=hd, s_=s_, a=a, kvv=kvv, cc=cc,
                         bk=bk: e.matmul(
                             pA(bk * 512, 512),
                             A(o_QT + hd * TOKB, TOKB)[:,
                               (4 * a + s_) * 128:(4 * a + s_ + 1) * 128],
                             Kf(kvv)[:, cc * 512:(cc + 1) * 512],
                             start=True, stop=True), [(s_pe, 1)])
                    mm1 = s_pe.n
                    P.w("vector", s_pe, mm1)
                    if cc == nch - 1:
                        P.w("vector", s_x, mld)
                        P.do("vector", lambda e, s_=s_, bk=bk: e.tensor_tensor(
                            pA(bk * 512, 512), pA(bk * 512, 512), m1b[s_][:],
                            op=ALU.add), [(s_dv, 1)])
                        P.w("vector", s_dv, s_dv.n)
                    P.do("vector", lambda e, s_=s_, cc=cc, bk=bk: e.reduce_max(
                        stat[:, 8 + 8 * s_ + cc:9 + 8 * s_ + cc],
                        pA(bk * 512, 512), axis=AX.X, negate=True), [(s_dv, 1)])
                    P.w("scalar", s_dv, s_dv.n)
                    P.do("scalar", lambda e, s_=s_, cc=cc, bk=bk: e.copy(
                        sstf(s_)[:, cc * 512:(cc + 1) * 512],
                        pA(bk * 512, 512)), [(s_sc, 1)])
                    ring.append((s_dv.n, s_sc.n))
                    if cc % 3 == 2 or cc == nch - 1:
                        yield
            last_psA_dv = s_dv.n
            phaseA_sc = s_sc.n
            P.w("vector", s_dv, s_dv.n)
            for s_ in range(4):
                if nchs[s_] > 1:
                    P.do("vector", lambda e, s_=s_, n=nchs[s_]: e.tensor_reduce(
                        stat[:, 40 + s_:41 + s_],
                        stat[:, 8 + 8 * s_:8 + 8 * s_ + n], axis=AX.X,
                        op=ALU.min), [(s_dv, 1)])
                else:
                    P.do("vector", lambda e, s_=s_: e.tensor_copy(
                        stat[:, 40 + s_:41 + s_], stat[:, 8 + 8 * s_:9 + 8 * s_]),
                        [(s_dv, 1)])
            P.w("vector", s_dv, s_dv.n)
            for s_ in range(4):
                P.do("vector", lambda e, s_=s_: e.tensor_scalar(
                    stat[:, 44 + s_:45 + s_], stat[:, 40 + s_:41 + s_], c.SC,
                    None, op0=ALU.mult), [(s_dv, 1)])
            nm_dv = s_dv.n
            P.w("scalar", s_dv, nm_dv)
            P.w("scalar", s_sc, phaseA_sc)
            exp_sc = [0, 0, 0, 0]
            l2 = []
            for s_ in range(4):
                nch = nchs[s_]
                parts = [(0, min(4, nch))] + ([(4, nch - 4)] if nch > 4 else [])
                for pi, (pc0, pn) in enumerate(parts):
                    P.do("scalar", lambda e, s_=s_, pc0=pc0, pn=pn, pi=pi:
                         e.activation(
                             dump[:, 0:pn * 512],
                             sstf(s_)[:, pc0 * 512:(pc0 + pn) * 512], AF.Exp,
                             scale=c.SC, bias=stat[:, 44 + s_:45 + s_],
                             accum_out=stat[:, (48 if pi == 0 else 56) +
                                            s_:(49 if pi == 0 else 57) + s_]),
                         [(s_sc, 1)])
                exp_sc[s_] = s_sc.n
                if len(parts) > 1:
                    l2.append(s_)
            yield
            if l2:
                P.w("vector", s_sc, exp_sc[l2[-1]])
                for s_ in l2:
                    P.do("vector", lambda e, s_=s_: e.tensor_tensor(
                        stat[:, 48 + s_:49 + s_], stat[:, 48 + s_:49 + s_],
                        stat[:, 56 + s_:57 + s_], op=ALU.add), [(s_dv, 1)])
                P.w("scalar", s_dv, s_dv.n)
            for s_ in range(4):
                P.w("scalar", s_sc, exp_sc[s_])
                P.do("scalar", lambda e, s_=s_: e.activation(
                    stat[:, 52 + s_:53 + s_], stat[:, 48 + s_:49 + s_], AF.Ln),
                    [(s_sc, 1)])
            ln_sc = s_sc.n
            P.w("vector", s_sc, ln_sc)
            for s_ in range(4):
                P.do("vector", lambda e, s_=s_: e.tensor_scalar(
                    stat[:, 52 + s_:53 + s_], stat[:, 52 + s_:53 + s_],
                    1.0 / c.SC, None, op0=ALU.mult), [(s_dv, 1)])
            P.w("vector", s_dv, s_dv.n)
            for s_ in range(4):
                P.do("vector", lambda e, s_=s_: e.tensor_tensor(
                    stat[:, 4 + s_:5 + s_], stat[:, 40 + s_:41 + s_],
                    stat[:, 52 + s_:53 + s_], op=ALU.subtract), [(s_dv, 1)])
            r_dv = s_dv.n
            yield
            P.w("tensor", s_dv, r_dv)
            P.do("tensor", lambda e: e.transpose(
                pA(1024, 128)[0:4, :], stat[:, 4:8], ident32[:]), [(s_pe, 1)])
            P.w("vector", s_pe, s_pe.n)
            if rr_dma:
                P.w("vector", s_rr, rr_dma[-1])
            P.do("vector", lambda e: e.tensor_copy(
                rr4[:], pA(1024, 128)[0:4, :]), [(s_dv, 1)])
            rr_cp.append(s_dv.n)
            P.w("scalar", s_dv, s_dv.n)
            P.do("scalar", lambda e: e.dma_start(out=rrow[0:1, :], in_=rr4[:]),
                 [(s_rr, 16)])
            rr_dma.append(s_rr.n)
            sl = hd % 2
            P.w("vector", s_rr, rr_dma[-1])
            P.w("vector", s_pe, s_pe.n)
            P.do("vector", lambda e, sl=sl: e.tensor_copy(rhis[sl][:], rrow[:]),
                 [(s_dv, 1)])
            P.w("vector", s_dv, s_dv.n)
            P.do("vector", lambda e, sl=sl: e.tensor_tensor(
                rlos[sl][:], rrow[:], rhis[sl][:], op=ALU.subtract), [(s_dv, 1)])
            bq_dv[hd] = s_dv.n

        def gen_pass2(a, hd):
            kvv = hd // (H // KVH)
            sl = hd % 2
            kt_max = min(8 * a + 8, c.SKT)
            P.w("tensor", s_dv, bq_dv[hd])
            P.w("tensor", s_dv, last_at_ev[0])
            P.w("tensor", s_x, kv_ld[kvv])
            P.w("tensor", s_x, vf_ld)
            t = 0
            pair = 0
            while t < kt_max:
                npr = min(2, kt_max - t)
                b0 = (pair % 2) * 1024
                if len(exp_done) >= 2:
                    P.w("tensor", s_sc, exp_done[-2])
                if b0 == 1024 and frep_sc:
                    P.w("tensor", s_sc, frep_sc[-1])
                for i in range(npr):
                    bank = pB(b0 + i * 512, 512)
                    P.do("tensor", lambda e, hd=hd, kvv=kvv, t=t, i=i, a=a,
                         bank=bank: e.matmul(
                             bank, Kf(kvv)[:, (t + i) * 128:(t + i + 1) * 128],
                             A(o_QT + hd * TOKB, TOKB)[:,
                               a * 512:(a + 1) * 512],
                             start=True, stop=False), ())
                    P.do("tensor", lambda e, bank=bank, sl=sl: e.matmul(
                        bank, onesrowb[:], rhis[sl][:], start=False,
                        stop=True), [(s_pe, 1)] if i == npr - 1 else ())
                mm2 = s_pe.n
                need_dv = False
                for i in range(npr):
                    d = t + i - 8 * a
                    if 0 <= d < 8:
                        P.w("vector", s_pe, mm2)
                        P.do("vector", lambda e, d=d, b=pB(b0 + i * 512, 512):
                             e.tensor_tensor(b, b, A(o_mp2 + d * 1024, 1024),
                                             op=ALU.add), [(s_dv, 1)])
                        need_dv = True
                if need_dv:
                    P.w("scalar", s_dv, s_dv.n)
                else:
                    P.w("scalar", s_pe, mm2)
                slot = pair % 2
                if len(av_mm) >= 2:
                    P.w("scalar", s_pe, av_mm[-2])
                P.do("scalar", lambda e, slot=slot, b0=b0, npr=npr: e.activation(
                    pT[slot][:, 0:npr * 512], pB(b0, npr * 512), AF.Exp,
                    scale=c.SC), [(s_sc, 1)])
                exp_done.append(s_sc.n)
                P.w("tensor", s_sc, s_sc.n)
                for i in range(npr):
                    P.do("tensor", lambda e, t=t, i=i, kvv=kvv, slot=slot,
                         kt_max=kt_max: e.matmul(
                             pA(1536, 512),
                             Vf[:, (t + i) * c.VW + kvv * 128:
                                (t + i) * c.VW + (kvv + 1) * 128],
                             pT[slot][:, i * 512:(i + 1) * 512],
                             start=(t + i == 0), stop=(t + i == kt_max - 1)),
                         [(s_pe, 1)] if i == npr - 1 else ())
                av_mm.append(s_pe.n)
                t += npr
                pair += 1
                yield
            P.w("tensor", s_sc, exp_done[-1])
            P.do("tensor", lambda e, sl=sl: e.matmul(
                pB(1024, 512), onesrowb[:], rlos[sl][:], start=True,
                stop=True), [(s_pe, 1)])
            P.w("scalar", s_pe, s_pe.n)
            P.do("scalar", lambda e: e.activation(
                A(o_frep, 2048, F32), pB(1024, 512), AF.Exp, scale=c.SC),
                [(s_sc, 1)])
            frep_sc.append(s_sc.n)
            P.w("vector", s_pe, av_mm[-1])
            P.w("vector", s_sc, frep_sc[-1])
            sl2 = (a * H + hd) % 2
            if len(at_st) >= 2:
                P.w("vector", s_st, at_st[-2])
            P.do("vector", lambda e, sl2=sl2: e.tensor_tensor(
                atev[sl2][:, 0:512], pA(1536, 512), A(o_frep, 2048, F32),
                op=ALU.mult), [(s_dv, 1)])
            last_at_ev[0] = s_dv.n
            P.w("sync", s_dv, s_dv.n)
            P.do("sync", lambda e, hd=hd, a=a, sl2=sl2: e.dma_start(
                out=attnTd[hd][:, a * 512:(a + 1) * 512],
                in_=atev[sl2][:, 0:512]), [(s_st, 16)])
            at_st.append(s_st.n)

        for a in range(c.NG):
            P.w("sync", s_dv, s_dv.n)
            P.w("sync", s_sc, s_sc.n)
            for d in range(8):
                P.do("sync", lambda e, a=a, d=d: e.dma_start(
                    out=A(o_mp2 + d * 1024, 1024), in_=mp2_d[a, d]), [(s_x, 16)])
            for s_ in range(4):
                P.do("sync", lambda e, a=a, s_=s_: e.dma_start(
                    out=m1b[s_][:], in_=mp1_d[4 * a + s_]), [(s_x, 16)])
            mld = s_x.n
            for hd in range(H):
                g1 = gen_pass1(a, hd, mld)
                g2 = gen_pass2(a, hd - 1) if hd > 0 else None
                alive = True
                while alive:
                    alive = False
                    try:
                        next(g1)
                        alive = True
                    except StopIteration:
                        pass
                    if g2 is not None:
                        try:
                            next(g2)
                            alive = True
                        except StopIteration:
                            pass
            for _ in gen_pass2(a, H - 1):
                pass
        if dbg:
            P.w("sync", s_st, s_st.n)
            P.do("sync", lambda e: e.dma_start(out=dbg_t["dattnT"], in_=attnTd),
                 [(s_st, 16)])
        return s_st.n

    def p5(at_total, x1_st):
        P.w("sync", s_st, at_total)
        for hd in range(H):
            P.do("sync", lambda e, hd=hd: e.dma_start(
                out=A(o_aT + hd * TOKB, TOKB), in_=attnTd[hd]), [(s_x, 16)])
        aT_ld = s_x.n
        for hk in range(H):
            P.do("gpsimd", lambda e, hk=hk: e.dma_start(
                out=A(o_wo + hk * DB, DB), in_=wo_d[hk * 128:(hk + 1) * 128, :]),
                [(s_wg, 16)])
        wo_ld = s_wg.n
        x1b = [A(o_x1b, D * 4, F32), A(o_x1b + D * 4, D * 4, F32)]
        u1 = A(o_u1, D * 4, F32)
        x2b = [A(o_x2b, D * 4, F32), A(o_x2b + D * 4, D * 4, F32)]
        P.w("tensor", s_x, aT_ld)
        P.w("tensor", s_wg, wo_ld)
        x2_dv, x2_st = [], []
        ld = []

        def p5_load(t):
            P.w("sync", s_st, x1_st)
            if t >= 2:
                P.w("sync", s_dv, x2_dv[t - 2])
            P.do("sync", lambda e, t=t: e.dma_start(
                out=x1b[t % 2][:], in_=x1d[t * 128:(t + 1) * 128, :]), [(s_x, 16)])
            ld.append(s_x.n)

        p5_load(0)
        for t in range(NT):
            if t + 1 < NT:
                p5_load(t + 1)
            x1_ld = ld[-1]
            ps = psA if t % 2 == 0 else psB
            if t >= 2:
                P.w("tensor", s_dv, x2_dv[t - 2])
            for hk in range(H):
                for m in range(NM):
                    P.do("tensor", lambda e, t=t, hk=hk, m=m, ps=ps: e.matmul(
                        ps[:, m * 512:(m + 1) * 512],
                        A(o_aT + hk * TOKB, TOKB)[:, t * 128:(t + 1) * 128],
                        A(o_wo + hk * DB, DB)[:, m * 512:(m + 1) * 512],
                        start=(hk == 0), stop=(hk == H - 1)),
                        [(s_pe, 1)] if (hk == H - 1 and m == NM - 1) else ())
            P.w("vector", s_pe, s_pe.n)
            P.w("vector", s_x, x1_ld)
            if t >= 2:
                P.w("vector", s_st, x2_st[t - 2])
            P.do("vector", lambda e, t=t: e.tensor_scalar(
                u1[:], x1b[t % 2][:], ab[:, 2 * t:2 * t + 1],
                ab[:, 2 * t + 1:2 * t + 2], op0=ALU.mult, op1=ALU.add),
                [(s_dv, 1)])
            P.w("vector", s_dv, s_dv.n)
            P.do("vector", lambda e, t=t, ps=ps: e.tensor_tensor(
                x2b[t % 2][:], u1[:], ps[:, 0:D], op=ALU.add), [(s_dv, 1)])
            x2_dv.append(s_dv.n)
            P.w("sync", s_dv, s_dv.n)
            P.do("sync", lambda e, t=t: e.dma_start(
                out=x2d[t * 128:(t + 1) * 128, :], in_=x2b[t % 2][:]), [(s_st, 16)])
            x2_st.append(s_st.n)
        if dbg:
            P.do("sync", lambda e: e.dma_start(out=dbg_t["dx2"], in_=x2d),
                 [(s_st, 16)])
        return s_st.n

    def p7(acc_total, x2_st_total):
        b1 = [A(o_7a, D * 4, F32), A(o_7a + D * 4, D * 4, F32)]
        b2 = [A(o_7b, D * 4, F32), A(o_7b + D * 4, D * 4, F32)]
        b3 = [A(o_7c, D * 4, F32), A(o_7c + D * 4, D * 4, F32)]
        P.w("sync", s_acc, acc_total)
        P.w("sync", s_st, x2_st_total)
        o_dv, o_st = [], []
        ld = []

        def p7_load(t):
            if t >= 2:
                P.w("sync", s_dv, o_dv[t - 2])
                P.w("sync", s_st, o_st[t - 2])
            P.do("sync", lambda e, t=t: e.dma_start(
                out=b1[t % 2][:], in_=x2d[t * 128:(t + 1) * 128, :]), [(s_x, 16)])
            P.do("sync", lambda e, t=t: e.dma_start(
                out=b2[t % 2][:], in_=accd[t * 128:(t + 1) * 128, :]), [(s_x, 16)])
            ld.append(s_x.n)

        p7_load(0)
        for t in range(NT):
            if t + 1 < NT:
                p7_load(t + 1)
            P.w("vector", s_x, ld[-1])
            P.do("vector", lambda e, t=t: e.tensor_tensor(
                b3[t % 2][:], b1[t % 2][:], b2[t % 2][:], op=ALU.add), [(s_dv, 1)])
            o_dv.append(s_dv.n)
            P.w("sync", s_dv, s_dv.n)
            P.do("sync", lambda e, t=t: e.dma_start(
                out=out_d[t * 128:(t + 1) * 128, :], in_=b3[t % 2][:]),
                [(s_st, 16)])
            o_st.append(s_st.n)
        return s_st.n

    p0()
    fence()
    f0_dv, f0_sc, _ = ffn_f0(x0, 0)
    acc1 = ffn_f1(f0_dv, f0_sc)
    if dbg:
        P.w("sync", s_acc, acc1)
        P.do("sync", lambda e: e.dma_start(out=dbg_t["dacc1"], in_=accd),
             [(s_st, 16)])
    dacc1_st = s_st.n
    fence()
    p2_dv, x1_st = p2(acc1)
    if dbg:
        P.w("sync", s_st, x1_st)
        P.do("sync", lambda e: e.dma_start(out=dbg_t["dx1"], in_=x1d),
             [(s_st, 16)])
    fence()
    qt_dv, kvf_ld = p3(p2_dv)
    fence()
    at_total = p4(qt_dv, kvf_ld)
    fence()
    x2_st = p5(at_total, x1_st)
    fence()
    f0_dv2, f0_sc2, _ = ffn_f0(x2d, x2_st)
    if dbg:
        P.w("gpsimd", s_st, dacc1_st)
    acc2 = ffn_f1(f0_dv2, f0_sc2)
    fence()
    final = p7(acc2, x2_st)
    P.w("scalar", s_st, final)
    P.w("scalar", s_acc, s_acc.n)

    with nc.Block() as block:
        block.sync(lambda e: P.replay("sync", e))
        block.gpsimd(lambda e: P.replay("gpsimd", e))
        block.tensor(lambda e: P.replay("tensor", e))
        block.vector(lambda e: P.replay("vector", e))
        block.scalar(lambda e: P.replay("scalar", e))

    stack.close()
    return nc


def make_masks(cfg, h):
    c = cfg
    NT = c.NT
    q1 = np.arange(128)
    mp1 = np.zeros((NT, 128, 512), np.float32)
    for j in range(NT):
        g = 2 * j + h
        a_last = ((g + 1) * 128 + 511) // 512 - 1
        kk = a_last * 512 + np.arange(512)
        qq = g * 128 + q1
        mp1[j] = np.where(kk[None, :] <= qq[:, None], 0.0, NEG)
    mp2 = np.zeros((c.NG, 8, 128, 512), np.float32)
    for a in range(c.NG):
        for d in range(8):
            t = 8 * a + d
            kk = t * 128 + q1
            qi = np.concatenate([(8 * a + 2 * s + h) * 128 + q1 for s in range(4)])
            if t >= c.SKT:
                mp2[a, d] = 0.0
            else:
                mp2[a, d] = np.where(kk[:, None] <= qi[None, :], 0.0, NEG)
    return mp1.astype(ml_dtypes.bfloat16), mp2.astype(ml_dtypes.bfloat16)


def make_in_maps(cfg, inputs, nonce):
    c = cfg
    x = np.ascontiguousarray(inputs["L0_attn_x"])
    wq = np.ascontiguousarray(inputs["L0_attn_Wq"])
    wk = np.ascontiguousarray(inputs["L0_attn_Wk"])
    wv = np.ascontiguousarray(inputs["L0_attn_Wv"])
    wo = np.ascontiguousarray(inputs["L0_attn_Wo"])
    gq = np.ascontiguousarray(inputs["L0_attn_q_gamma"]).reshape(1, 128)
    gk = np.ascontiguousarray(inputs["L0_attn_k_gamma"]).reshape(1, 128)
    r9w = np.ascontiguousarray(np.concatenate(
        [inputs["L0_ffn_router_w"].T,
         inputs["L0_ffn_shared_gate_w"].reshape(-1, 1)], axis=1))
    wgate = np.ascontiguousarray(inputs["L0_ffn_w_gate"])
    wup = np.ascontiguousarray(inputs["L0_ffn_w_up"])
    wdown = np.ascontiguousarray(inputs["L0_ffn_w_down"])
    swg = np.ascontiguousarray(inputs["L0_ffn_sw_gate"])
    swu = np.ascontiguousarray(inputs["L0_ffn_sw_up"])
    swd = np.ascontiguousarray(inputs["L0_ffn_sw_down"])
    ident = np.eye(128, dtype=np.float32)
    identb = ident.astype(ml_dtypes.bfloat16)
    in_maps = []
    for core in range(8):
        b, h = core // 2, core % 2
        xs = np.ascontiguousarray(
            x[b].reshape(2 * c.NT, 128, c.D)[h::2].reshape(c.TOK, c.D))
        mp1, mp2 = make_masks(cfg, h)
        in_maps.append({
            "x0": xs, "wq": wq, "wk": wk, "wv": wv, "wo": wo,
            "gq": gq, "gk": gk, "r9w": r9w,
            "wgate": wgate, "wup": wup, "wdown": wdown,
            "swg": swg, "swu": swu, "swd": swd,
            "mp1": mp1, "mp2": mp2,
            "hsel": np.full((1, 64), h, dtype=np.int32),
            "noncein": np.full((1, 64), nonce, dtype=np.int32),
            "identb": identb, "ident32": ident,
        })
    return in_maps


def kernel(**inputs):
    cfg = FULL
    nonce = np.int32(np.random.default_rng().integers(1, 2**31 - 1))
    in_maps = make_in_maps(cfg, inputs, nonce)
    nc = build(cfg)
    res = run_bass_kernel_spmd(nc, in_maps, core_ids=list(range(8)))
    out = np.empty((4, 2 * cfg.TOK, cfg.D), np.float32)
    for core in range(8):
        b, h = core // 2, core % 2
        o = res.results[core]["out"].reshape(cfg.NT, 128, cfg.D)
        out[b].reshape(2 * cfg.NT, 128, cfg.D)[h::2] = o
    return out


# version 16
# speedup vs baseline: 1.2412x; 1.2412x over previous
import numpy as np
import ml_dtypes
import concourse.bass as bass
import concourse.mybir as mybir
from concourse.bass_utils import run_bass_kernel_spmd
from contextlib import ExitStack

F32 = mybir.dt.float32
BF16 = mybir.dt.bfloat16
I32 = mybir.dt.int32
AF = mybir.ActivationFunctionType
ALU = mybir.AluOpType
AX = mybir.AxisListType

NEG = -1.0e30


class Cfg:
    def __init__(self, KT=16, NT=16, E=8, NIT=11, NSH=4, H=16, KVH=4):
        self.KT, self.NT, self.E, self.NIT, self.NSH = KT, NT, E, NIT, NSH
        self.H, self.KVH = H, KVH
        self.D = KT * 128
        self.TOK = NT * 128
        self.S = 2 * self.TOK
        self.SKT = self.S // 128
        self.NSEG = max(1, self.TOK // 512)
        self.SEG = min(512, self.TOK)
        self.NG = NT // 4
        self.I = NIT * 128
        self.SI = NSH * NIT * 128
        self.VW = KVH * 128
        self.SC = 128 ** -0.5
        assert NT % 4 == 0 and self.TOK % 512 == 0 or NT == 4


FULL = Cfg()


class Sm:
    __slots__ = ("h", "n", "name")

    def __init__(self, h, name):
        self.h, self.n, self.name = h, 0, name


class Prog:
    ENGS = ("sync", "gpsimd", "tensor", "vector", "scalar")

    def __init__(self):
        self.q = {e: [] for e in self.ENGS}
        self.lastw = {e: {} for e in self.ENGS}

    def w(self, eng, sem, val):
        val = int(val)
        if val <= 0:
            return
        if self.lastw[eng].get(sem.name, -1) >= val:
            return
        self.lastw[eng][sem.name] = val
        self.q[eng].append(("w", sem, val))

    def do(self, eng, fn, incs=()):
        for s, k in incs:
            s.n += k
        self.q[eng].append(("c", fn, tuple(incs)))

    def raw(self, eng, fn):
        self.q[eng].append(("r", fn))

    def replay(self, eng, obj):
        for item in self.q[eng]:
            if item[0] == "w":
                obj.wait_ge(item[1].h, item[2])
            elif item[0] == "c":
                ins = item[1](obj)
                for s, k in item[2]:
                    ins.then_inc(s.h, k)
            else:
                item[1](obj)


def build(cfg, dbg=False):
    nc = bass.Bass("TRN2", target_bir_lowering=False)
    vh_cell = {}
    c = cfg
    D, TOK, NT, KT, H, KVH = c.D, c.TOK, c.NT, c.KT, c.H, c.KVH
    TOKB, DB = TOK * 2, D * 2
    NM = D // 512
    NCH = TOK // 512 if TOK >= 512 else 1

    x0 = nc.dram_tensor("x0", [TOK, D], F32, kind="ExternalInput").ap()
    wq_d = nc.dram_tensor("wq", [D, H * 128], F32, kind="ExternalInput").ap()
    wk_d = nc.dram_tensor("wk", [D, KVH * 128], F32, kind="ExternalInput").ap()
    wv_d = nc.dram_tensor("wv", [D, KVH * 128], F32, kind="ExternalInput").ap()
    wo_d = nc.dram_tensor("wo", [H * 128, D], F32, kind="ExternalInput").ap()
    gq_d = nc.dram_tensor("gq", [1, 128], F32, kind="ExternalInput").ap()
    gk_d = nc.dram_tensor("gk", [1, 128], F32, kind="ExternalInput").ap()
    r9_d = nc.dram_tensor("r9w", [D, c.E + 1], F32, kind="ExternalInput").ap()
    wg_d = nc.dram_tensor("wgate", [c.E, D, c.I], F32, kind="ExternalInput").ap()
    wu_d = nc.dram_tensor("wup", [c.E, D, c.I], F32, kind="ExternalInput").ap()
    wd_d = nc.dram_tensor("wdown", [c.E, c.I, D], F32, kind="ExternalInput").ap()
    swg_d = nc.dram_tensor("swg", [D, c.SI], F32, kind="ExternalInput").ap()
    swu_d = nc.dram_tensor("swu", [D, c.SI], F32, kind="ExternalInput").ap()
    swd_d = nc.dram_tensor("swd", [c.SI, D], F32, kind="ExternalInput").ap()
    mp1_d = nc.dram_tensor("mp1", [NT, 128, 512], BF16, kind="ExternalInput").ap()
    mp2_d = nc.dram_tensor("mp2", [c.NG, 8, 128, 512], BF16, kind="ExternalInput").ap()
    hsel_d = nc.dram_tensor("hsel", [1, 64], I32, kind="ExternalInput").ap()
    nonce_d = nc.dram_tensor("noncein", [1, 64], I32, kind="ExternalInput").ap()
    idb_d = nc.dram_tensor("identb", [128, 128], BF16, kind="ExternalInput").ap()
    id32_d = nc.dram_tensor("ident32", [128, 128], F32, kind="ExternalInput").ap()

    out_d = nc.dram_tensor("out", [TOK, D], F32, kind="ExternalOutput").ap()

    x1d = nc.dram_tensor("x1d", [TOK, D], F32, kind="Internal").ap()
    x2d = nc.dram_tensor("x2d", [TOK, D], F32, kind="Internal").ap()
    accd = nc.dram_tensor("accd", [TOK, D], F32, kind="Internal").ap()
    attnTd = nc.dram_tensor("attnTd", [H, 128, TOK], BF16, kind="Internal").ap()
    kvh_elems = KVH * 128 * TOK + NT * 128 * c.VW
    kv_mine = nc.dram_tensor("kv_mine", [kvh_elems], BF16, kind="Internal").ap()
    sh_kv = nc.dram_tensor("sh_kv", [2, kvh_elems], BF16, kind="Internal",
                           addr_space="Shared").ap()
    sh_flag = nc.dram_tensor("sh_flag", [2, 64], I32, kind="Internal",
                             addr_space="Shared").ap()
    dbg_t = {}
    if dbg:
        for nm, shape, dt in [("dacc1", [TOK, D], F32), ("dx1", [TOK, D], F32),
                              ("dattnT", [H, 128, TOK], BF16),
                              ("dx2", [TOK, D], F32),
                              ("dkv", [2, kvh_elems], BF16)]:
            dbg_t[nm] = nc.dram_tensor(nm, shape, dt, kind="ExternalOutput").ap()

    P = Prog()
    stack = ExitStack()
    ent = stack.enter_context

    s_x = Sm(ent(nc.semaphore("s_x")), "s_x")
    s_st = Sm(ent(nc.semaphore("s_st")), "s_st")
    s_wg = Sm(ent(nc.semaphore("s_wg")), "s_wg")
    s_acc = Sm(ent(nc.semaphore("s_acc")), "s_acc")
    s_kvw = Sm(ent(nc.semaphore("s_kvw")), "s_kvw")
    s_bar = Sm(ent(nc.semaphore("s_bar")), "s_bar")
    s_pe = Sm(ent(nc.semaphore("s_pe")), "s_pe")
    s_dv = Sm(ent(nc.semaphore("s_dv")), "s_dv")
    s_sc = Sm(ent(nc.semaphore("s_sc")), "s_sc")
    s_gb = Sm(ent(nc.semaphore("s_gb")), "s_gb")
    s_rr = Sm(ent(nc.semaphore("s_rr")), "s_rr")

    identb = ent(nc.sbuf_tensor("sb_identb", [128, 128], BF16))
    ident32 = ent(nc.sbuf_tensor("sb_ident32", [128, 128], F32))
    ones128b = ent(nc.sbuf_tensor("ones128b", [128, 128], BF16))
    onesrow32 = ent(nc.sbuf_tensor("onesrow32", [1, 128], F32))
    grow = ent(nc.sbuf_tensor("grow", [1, 256], F32))
    gpcol = ent(nc.sbuf_tensor("gpcol", [128, 8], F32))
    r9 = ent(nc.sbuf_tensor("r9", [128, KT * (c.E + 1)], BF16))
    comb = ent(nc.sbuf_tensor("comb", [128, NT * c.E], F32))
    gcol = ent(nc.sbuf_tensor("gcol", [128, NT], F32))
    ab = ent(nc.sbuf_tensor("ab", [128, 2 * NT], F32))
    ivec = ent(nc.sbuf_tensor("ivec", [1, 128], I32))
    fpoll = ent(nc.sbuf_tensor("fpoll", [1, 128], I32))
    stat = ent(nc.sbuf_tensor("stat", [128, 64], F32))
    rrow = ent(nc.sbuf_tensor("rrow", [1, 512], F32))
    rr4 = ent(nc.sbuf_tensor("rr4", [4, 128], F32))
    onesrowb = ent(nc.sbuf_tensor("onesrowb", [1, 128], BF16))
    rhis = [ent(nc.sbuf_tensor("rhi0", [1, 512], BF16)),
            ent(nc.sbuf_tensor("rhi1", [1, 512], BF16))]
    rlos = [ent(nc.sbuf_tensor("rlo0", [1, 512], BF16)),
            ent(nc.sbuf_tensor("rlo1", [1, 512], BF16))]
    tiny = ent(nc.sbuf_tensor("tiny", [128, 32], F32))
    r_h = ent(nc.gpsimd.register("r_h"))

    AR_B = 199680
    arena = ent(nc.sbuf_tensor("arena", [128, AR_B // 2], BF16))

    def A(off, nbytes, dtype=BF16):
        assert off % 64 == 0 and off + nbytes <= AR_B, (off, nbytes)
        ap = arena[:, off // 2:(off + nbytes) // 2]
        return ap if dtype == BF16 else ap.bitcast(dtype)

    psA = ent(nc.psum_tensor("psA", [128, 2048], F32))
    psB = ent(nc.psum_tensor("psB", [128, 2048], F32))

    def pA(c0, n):
        return psA[:, c0:c0 + n]

    def pB(c0, n):
        return psB[:, c0:c0 + n]

    def pAb(c0f32, nb):
        return psA.bitcast(BF16)[:, 2 * c0f32:2 * c0f32 + nb]

    def pBb(c0f32, nb):
        return psB.bitcast(BF16)[:, 2 * c0f32:2 * c0f32 + nb]

    o_xbuf = 0
    o_wgu = 0
    wgu_sz = 2 * KT * 256
    o_ybuf = 2 * D * 4
    o_stage = o_ybuf + 2 * DB
    o_yT = o_stage + 2 * 4096
    o_h = o_yT + KT * TOKB
    o_wd = o_h + c.NIT * TOKB
    ffn_end = o_wd + c.NIT * DB
    assert ffn_end <= AR_B, ffn_end
    assert 2 * wgu_sz <= o_ybuf

    o_y2T = o_yT
    o_QT = o_h
    o_wkv = o_QT + H * TOKB
    wkv_sz = KT * c.VW * 2
    p3_end = o_wkv + 2 * wkv_sz
    assert p3_end <= AR_B, p3_end
    o_wqr = 0
    o_qsq = o_wqr + 2 * KT * 256
    o_rsd = o_qsq + 2048
    o_evk = o_rsd + 2 * 2048
    assert o_evk + 2 * 1024 <= o_yT

    P4S = 8 * 1024 + 2 * 4096 + 4 * 1024 + 4096 + 2 * 1024 + 2048
    if P4S <= min(o_yT, o_QT):
        o_p4s = 0
        o_Kf = o_yT
        kf_lim = o_QT
    else:
        o_Kf = ((o_QT + H * TOKB + 63) // 64) * 64
        o_p4s = o_Kf + KVH * c.S * 2 + c.SKT * c.VW * 2
        kf_lim = AR_B
    o_mp2 = o_p4s
    o_pT = o_mp2 + 8 * 1024
    o_mp1 = o_pT + 2 * 4096
    o_dump = o_mp1 + 4 * 1024
    o_atev = o_dump + 4096
    o_frep = o_atev + 2 * 1024
    o_Vf = o_Kf + KVH * c.S * 2
    p4_end = max(o_Vf + c.SKT * c.VW * 2, o_frep + 2048)
    assert p4_end <= kf_lim or o_p4s > 0, (p4_end, kf_lim)
    assert o_frep + 2048 <= AR_B
    o_sst = ((max(o_QT + H * TOKB, p4_end, o_frep + 2048) + 63) // 64) * 64
    sst_bytes = sum((4 * (c.NG - 1) + s_ + 2) // 2 for s_ in range(4)) * 1024
    assert o_sst + sst_bytes <= AR_B, (o_sst, sst_bytes)

    o_aT = 0
    o_wo = o_aT + H * TOKB
    o_x1b = o_wo + H * DB
    o_u1 = o_x1b + 2 * D * 4
    o_x2b = o_u1 + D * 4
    assert o_x2b + 2 * D * 4 <= AR_B

    o_7a = 0
    o_7b = 2 * D * 4
    o_7c = 4 * D * 4

    def fence():
        for eng in Prog.ENGS:
            for sm in (s_pe, s_dv, s_sc, s_x, s_st, s_wg, s_acc, s_kvw):
                P.w(eng, sm, sm.n)

    def p0():
        P.do("sync", lambda e: e.dma_start(out=identb[:], in_=idb_d), [(s_x, 16)])
        P.do("sync", lambda e: e.dma_start(out=ident32[:], in_=id32_d), [(s_x, 16)])
        P.do("sync", lambda e: e.dma_start(out=grow[0:1, 0:128], in_=gq_d), [(s_x, 16)])
        P.do("sync", lambda e: e.dma_start(out=grow[0:1, 128:256], in_=gk_d), [(s_x, 16)])
        P.do("gpsimd", lambda e: e.dma_start(out=ivec[0:1, 0:64], in_=hsel_d), [(s_wg, 16)])
        P.do("gpsimd", lambda e: e.dma_start(out=ivec[0:1, 64:128], in_=nonce_d), [(s_wg, 16)])
        P.do("gpsimd", lambda e: e.dma_start(
            out=r9[:], in_=r9_d.rearrange("(kt p) c -> p kt c", p=128)), [(s_wg, 16)])
        P.w("gpsimd", s_wg, 48)

        def _loadh(e):
            e.reg_load(r_h, ivec[0:1, 0:1])
            vh_cell["v"] = e.snap(r_h, min_val=0, max_val=1)
            return e.nop()
        P.do("gpsimd", _loadh, [(s_gb, 1)])
        P.w("vector", s_x, 64)
        P.do("vector", lambda e: e.memset(ones128b[:], 1.0), ())
        P.do("vector", lambda e: e.memset(onesrow32[:], 1.0), ())
        P.do("vector", lambda e: e.memset(onesrowb[:], 1.0), ())
        P.do("vector", lambda e: e.memset(stat[:, 60:61], 1e-5), ())
        P.do("vector", lambda e: e.memset(stat[:, 61:62], 1e-8), [(s_dv, 1)])
        P.do("vector", lambda e: e.tensor_tensor(
            grow[0:1, 0:128], grow[0:1, 0:128], grow[0:1, 128:256], op=ALU.mult),
            [(s_dv, 1)])
        P.w("tensor", s_dv, s_dv.n)
        P.do("tensor", lambda e: e.matmul(
            pA(0, 1), grow[0:1, 0:128], onesrow32[0:1, 0:1],
            start=True, stop=True), [(s_pe, 1)])
        P.w("scalar", s_pe, s_pe.n)
        P.do("scalar", lambda e: e.copy(gpcol[:, 0:1], pA(0, 1)),
             [(s_sc, 1)])
        return s_sc.n, s_dv.n

    def ln_tile(t, xb, ybf, sq, yT_off, extra_wait_pe=0):
        P.do("vector", lambda e: e.reduce_sum(tiny[:, 0:1], xb[:], axis=AX.X),
             [(s_dv, 1)])
        P.w("scalar", s_dv, s_dv.n)
        P.do("scalar", lambda e: e.mul(tiny[:, 1:2], tiny[:, 0:1], 1.0 / D),
             [(s_sc, 1)])
        P.w("vector", s_sc, s_sc.n)
        P.do("vector", lambda e: e.tensor_scalar(
            xb[:], xb[:], tiny[:, 1:2], None, op0=ALU.subtract), [(s_dv, 1)])
        P.w("scalar", s_dv, s_dv.n)
        P.do("scalar", lambda e: e.activation(
            sq[:], xb[:], AF.Square, accum_out=tiny[:, 2:3]), [(s_sc, 1)])
        P.w("scalar", s_sc, s_sc.n)
        P.do("scalar", lambda e: e.activation(
            tiny[:, 3:4], tiny[:, 2:3], AF.Ln, scale=1.0 / D,
            bias=stat[:, 60:61]), [(s_sc, 1)])
        P.w("scalar", s_sc, s_sc.n)
        P.do("scalar", lambda e: e.activation(
            tiny[:, 3:4], tiny[:, 3:4], AF.Exp, scale=-0.5), [(s_sc, 1)])
        P.w("scalar", s_sc, s_sc.n)
        if extra_wait_pe:
            P.w("scalar", s_pe, extra_wait_pe)
        P.do("scalar", lambda e: e.activation(
            ybf[:], xb[:], AF.Copy, scale=tiny[:, 3:4]), [(s_sc, 1)])
        y_sc = s_sc.n
        evs = []
        for k in range(KT):
            P.w("tensor", s_sc, y_sc)
            slot = pAb(0, 128) if k % 2 == 0 else pBb(0, 128)
            if len(ln_tr_ev) >= 2:
                P.w("tensor", s_dv, ln_tr_ev[-2])
            P.do("tensor", lambda e, k=k, slot=slot: e.transpose(
                slot, ybf[:, k * 128:(k + 1) * 128], identb[:]), [(s_pe, 1)])
            P.w("vector", s_pe, s_pe.n)
            P.do("vector", lambda e, k=k, slot=slot: e.tensor_copy(
                A(yT_off + k * TOKB, TOKB)[:, t * 128:(t + 1) * 128], slot),
                [(s_dv, 1)])
            ln_tr_ev.append(s_dv.n)
            evs.append(s_dv.n)
        return y_sc, evs, s_pe.n

    ln_tr_ev = []

    def ffn_f0(src, src_ready_st):
        xb = [A(o_xbuf, D * 4, F32), A(o_xbuf + D * 4, D * 4, F32)]
        yb = [A(o_ybuf, DB), A(o_ybuf + DB, DB)]
        sq = A(o_stage, D * 4, F32)
        y_done, tr_pe, comb_done, gate_sc = [], [], [], []
        ld = []

        def f0_load(t):
            if src_ready_st:
                P.w("sync", s_st, src_ready_st)
            if t >= 2:
                P.w("sync", s_sc, y_done[t - 2])
            P.do("sync", lambda e, t=t: e.dma_start(
                out=xb[t % 2][:], in_=src[t * 128:(t + 1) * 128, :]), [(s_x, 16)])
            ld.append(s_x.n)

        f0_load(0)
        for t in range(NT):
            if t + 1 < NT:
                f0_load(t + 1)
            P.w("vector", s_x, ld[-1])
            ysc, evs, pen = ln_tile(t, xb[t % 2], yb[t % 2], sq, o_yT,
                                    extra_wait_pe=(tr_pe[t - 2] if t >= 2 else 0))
            y_done.append(ysc)
            tr_pe.append(pen)
            ne1 = c.E + 1
            psR = pA(512, ne1)
            if t >= 1:
                P.w("tensor", s_dv, comb_done[t - 1])
                P.w("tensor", s_sc, gate_sc[t - 1])
            for k in range(KT):
                P.w("tensor", s_dv, evs[k])
                P.do("tensor", lambda e, k=k, t=t: e.matmul(
                    psR, A(o_yT + k * TOKB, TOKB)[:, t * 128:(t + 1) * 128],
                    r9[:, k * ne1:(k + 1) * ne1], start=(k == 0),
                    stop=(k == KT - 1)),
                    [(s_pe, 1)] if k == KT - 1 else ())
            P.w("vector", s_pe, s_pe.n)
            L = psR[:, 0:c.E]

            def ve(fn):
                P.do("vector", fn, [(s_dv, 1)])
                P.w("vector", s_dv, s_dv.n)

            ve(lambda e: e.reduce_max(tiny[:, 4:5], L, axis=AX.X, negate=True))
            ve(lambda e: e.tensor_scalar(
                stat[:, 0:c.E], L, tiny[:, 4:5], None, op0=ALU.add))
            P.w("scalar", s_dv, s_dv.n)
            P.do("scalar", lambda e: e.activation(
                stat[:, 8:8 + c.E], stat[:, 0:c.E], AF.Exp), [(s_sc, 1)])
            ve(lambda e: e.tensor_scalar(
                stat[:, 16:16 + c.E], stat[:, 0:c.E], 0.0, None, op0=ALU.is_ge))
            ve(lambda e: e.scalar_tensor_tensor(
                stat[:, 16:16 + c.E], stat[:, 16:16 + c.E], NEG, stat[:, 0:c.E],
                op0=ALU.mult, op1=ALU.add))
            ve(lambda e: e.reduce_max(
                tiny[:, 5:6], stat[:, 16:16 + c.E], axis=AX.X, negate=True))
            ve(lambda e: e.tensor_scalar(
                stat[:, 16:16 + c.E], stat[:, 0:c.E], tiny[:, 5:6], 0.0,
                op0=ALU.add, op1=ALU.is_ge))
            P.w("vector", s_sc, s_sc.n)
            ve(lambda e: e.tensor_tensor(
                stat[:, 16:16 + c.E], stat[:, 16:16 + c.E], stat[:, 8:8 + c.E],
                op=ALU.mult))
            ve(lambda e: e.reduce_sum(
                tiny[:, 6:7], stat[:, 16:16 + c.E], axis=AX.X))
            ve(lambda e: e.reciprocal(tiny[:, 7:8], tiny[:, 6:7]))
            P.w("scalar", s_dv, s_dv.n)
            P.do("scalar", lambda e: e.mul(tiny[:, 7:8], tiny[:, 7:8], 0.5),
                 [(s_sc, 1)])
            P.w("vector", s_sc, s_sc.n)
            P.do("vector", lambda e, t=t: e.tensor_scalar(
                comb[:, t * c.E:(t + 1) * c.E], stat[:, 16:16 + c.E],
                tiny[:, 7:8], None, op0=ALU.mult), [(s_dv, 1)])
            comb_done.append(s_dv.n)
            P.do("scalar", lambda e: e.activation(
                tiny[:, 9:10], psR[:, c.E:c.E + 1], AF.Exp, scale=-1.0),
                [(s_sc, 1)])
            P.w("scalar", s_sc, s_sc.n)
            P.do("scalar", lambda e: e.add(tiny[:, 9:10], tiny[:, 9:10], 1.0),
                 [(s_sc, 1)])
            gate_sc.append(s_sc.n)
            P.w("vector", s_sc, s_sc.n)
            ve(lambda e: e.reciprocal(tiny[:, 10:11], tiny[:, 9:10]))
            P.do("vector", lambda e, t=t: e.tensor_scalar(
                gcol[:, t:t + 1], tiny[:, 10:11], 0.5, None, op0=ALU.mult),
                [(s_dv, 1)])
            comb_done[-1] = s_dv.n
        return s_dv.n, s_sc.n, s_pe.n

    def ffn_f1(f0_dv, f0_sc):
        units = []
        for e_ in range(c.E):
            units.append((wg_d[e_], wu_d[e_], wd_d[e_],
                          lambda t, e_=e_: comb[:, t * c.E + e_:t * c.E + e_ + 1]))
        for cc in range(c.NSH):
            i0 = cc * c.I
            units.append((swg_d[:, i0:i0 + c.I], swu_d[:, i0:i0 + c.I],
                          swd_d[i0:i0 + c.I, :], lambda t: gcol[:, t:t + 1]))
        wgu = [A(o_wgu, wgu_sz), A(o_wgu + wgu_sz, wgu_sz)]
        stg = [A(o_stage, 4096, F32), A(o_stage + 4096, 4096, F32)]

        gu_need, gu_done = {}, {}
        evac_keys = []
        evac_dv = {}
        down_ev = []
        stage_acc = []
        seq = 0

        def emit_wgu_load(u, iT, wgs, wus):
            g = seq + iT
            if g >= 2:
                P.w("gpsimd", s_pe, gu_done[g - 2])
            else:
                P.w("gpsimd", s_sc, f0_sc)
            ic = slice(iT * 128, (iT + 1) * 128)
            P.do("gpsimd", lambda e, wgs=wgs, ic=ic, g=g: e.dma_start(
                out=wgu[g % 2][:, 0:KT * 128],
                in_=wgs[:, ic].rearrange("(kt p) i -> p kt i", p=128)),
                [(s_wg, 16)])
            P.do("gpsimd", lambda e, wus=wus, ic=ic, g=g: e.dma_start(
                out=wgu[g % 2][:, KT * 128:2 * KT * 128],
                in_=wus[:, ic].rearrange("(kt p) i -> p kt i", p=128)),
                [(s_wg, 16)])
            gu_need[g] = s_wg.n

        for u, (wgs, wus, wds, scf) in enumerate(units):
            for iT in range(min(2, c.NIT)):
                emit_wgu_load(u, iT, wgs, wus)
            if u > 0:
                P.w("gpsimd", s_pe, down_done)
            for iT in range(c.NIT):
                P.do("gpsimd", lambda e, wds=wds, iT=iT: e.dma_start(
                    out=A(o_wd + iT * DB, DB),
                    in_=wds[iT * 128:(iT + 1) * 128, :]), [(s_wg, 16)])
            wd_need = s_wg.n

            if u == 0:
                P.w("tensor", s_dv, f0_dv)
                P.w("tensor", s_sc, f0_sc)
            else:
                P.w("tensor", s_dv, down_ev[-1])
            for iT in range(c.NIT):
                g = seq + iT
                P.w("tensor", s_wg, gu_need[max(g, min(seq + 1, seq + c.NIT - 1))]
                    if iT < 2 else gu_need[g])
                sg = 0
                while sg < c.NSEG:
                    npair = min(2, c.NSEG - sg)
                    par = len(evac_keys) % 2
                    gbanks = [pA((par * 2 + i) % 4 * 512, c.SEG)
                              for i in range(npair)]
                    ubanks = [pB((par * 2 + i) % 4 * 512, c.SEG)
                              for i in range(npair)]
                    ek = (u, iT, sg)
                    if len(evac_keys) >= 2:
                        P.w("tensor", s_dv, evac_dv[evac_keys[-2]])
                    tcs = [slice((sg + i) * c.SEG, (sg + i + 1) * c.SEG)
                           for i in range(npair)]
                    for k in range(KT):
                        for i in range(npair):
                            P.do("tensor", lambda e, k=k, g=g, b=gbanks[i],
                                 tc=tcs[i]: e.matmul(
                                     b, wgu[g % 2][:, k * 128:(k + 1) * 128],
                                     A(o_yT + k * TOKB, TOKB)[:, tc],
                                     start=(k == 0), stop=(k == KT - 1)), ())
                        for i in range(npair):
                            P.do("tensor", lambda e, k=k, g=g, b=ubanks[i],
                                 tc=tcs[i]: e.matmul(
                                     b, wgu[g % 2][:, KT * 128 + k * 128:
                                                   KT * 128 + (k + 1) * 128],
                                     A(o_yT + k * TOKB, TOKB)[:, tc],
                                     start=(k == 0), stop=(k == KT - 1)),
                                 [(s_pe, 1)] if (k == KT - 1 and
                                                 i == npair - 1) else ())
                    P.w("scalar", s_pe, s_pe.n)
                    for i in range(npair):
                        P.do("scalar", lambda e, iT=iT, b=gbanks[i], tc=tcs[i]:
                             e.activation(A(o_h + iT * TOKB, TOKB)[:, tc], b,
                                          AF.Silu),
                             [(s_sc, 1)] if i == npair - 1 else ())
                    P.w("vector", s_sc, s_sc.n)
                    for i in range(npair):
                        P.do("vector", lambda e, iT=iT, b=ubanks[i], tc=tcs[i]:
                             e.tensor_tensor(A(o_h + iT * TOKB, TOKB)[:, tc],
                                             A(o_h + iT * TOKB, TOKB)[:, tc],
                                             b, op=ALU.mult),
                             [(s_dv, 1)] if i == npair - 1 else ())
                    evac_keys.append(ek)
                    evac_dv[ek] = s_dv.n
                    sg += npair
                gu_done[g] = s_pe.n
                if iT + 2 < c.NIT:
                    emit_wgu_load(u, iT + 2, wgs, wus)
            seq += c.NIT
            gu_all = s_dv.n

            P.w("tensor", s_wg, wd_need)
            P.w("tensor", s_dv, gu_all)
            for t in range(NT):
                ps = psA if t % 2 == 0 else psB
                if len(down_ev) >= 2:
                    P.w("tensor", s_dv, down_ev[-2])
                for iT in range(c.NIT):
                    for m in range(NM):
                        P.do("tensor", lambda e, t=t, iT=iT, m=m, ps=ps: e.matmul(
                            ps[:, m * 512:(m + 1) * 512],
                            A(o_h + iT * TOKB, TOKB)[:, t * 128:(t + 1) * 128],
                            A(o_wd + iT * DB, DB)[:, m * 512:(m + 1) * 512],
                            start=(iT == 0), stop=(iT == c.NIT - 1)),
                            [(s_pe, 1)] if (iT == c.NIT - 1 and m == NM - 1)
                            else ())
                dmm = s_pe.n
                npc = max(1, D // 1024)
                cw = min(1024, D)
                for mh in range(npc):
                    slot = len(stage_acc) % 2
                    if len(stage_acc) >= 2:
                        P.w("vector", s_acc, stage_acc[-2])
                    P.w("vector", s_pe, dmm)
                    P.do("vector", lambda e, t=t, mh=mh, ps=ps, slot=slot, cw=cw,
                         scf=scf: e.tensor_scalar(
                             stg[slot][:, 0:cw], ps[:, mh * cw:(mh + 1) * cw],
                             scf(t), None, op0=ALU.mult), [(s_dv, 1)])
                    P.w("gpsimd", s_dv, s_dv.n)
                    kw = {} if u == 0 else {"accum_op": ALU.add}
                    P.do("gpsimd", lambda e, t=t, mh=mh, slot=slot, cw=cw, kw=kw:
                         e.dma_start(out=accd[t * 128:(t + 1) * 128,
                                              mh * cw:(mh + 1) * cw],
                                     in_=stg[slot][:, 0:cw], **kw), [(s_acc, 16)])
                    stage_acc.append(s_acc.n)
                down_ev.append(s_dv.n)
            down_done = s_pe.n
        return s_acc.n

    def p2(acc_total):
        xb = [A(o_xbuf, D * 4, F32), A(o_xbuf + D * 4, D * 4, F32)]
        ac = [A(o_h, D * 4, F32), A(o_h + D * 4, D * 4, F32)]
        yb = [A(o_ybuf, DB), A(o_ybuf + DB, DB)]
        sq = A(o_stage, D * 4, F32)
        P.w("sync", s_acc, acc_total)
        y_done, st_list, tr_pe = [], [], []
        ld = []

        def p2_load(t):
            if t >= 2:
                P.w("sync", s_sc, y_done[t - 2])
            P.do("sync", lambda e, t=t: e.dma_start(
                out=xb[t % 2][:], in_=x0[t * 128:(t + 1) * 128, :]), [(s_x, 16)])
            P.do("sync", lambda e, t=t: e.dma_start(
                out=ac[t % 2][:], in_=accd[t * 128:(t + 1) * 128, :]), [(s_x, 16)])
            ld.append(s_x.n)

        p2_load(0)
        for t in range(NT):
            if t + 1 < NT:
                p2_load(t + 1)
            P.w("vector", s_x, ld[-1])
            P.do("vector", lambda e, t=t: e.tensor_tensor(
                xb[t % 2][:], xb[t % 2][:], ac[t % 2][:], op=ALU.add), [(s_dv, 1)])
            P.w("sync", s_dv, s_dv.n)
            P.do("sync", lambda e, t=t: e.dma_start(
                out=x1d[t * 128:(t + 1) * 128, :], in_=xb[t % 2][:]), [(s_st, 16)])
            st_list.append(s_st.n)
            P.w("vector", s_st, s_st.n)
            ysc, evs, pen = ln_tile(t, xb[t % 2], yb[t % 2], sq, o_y2T,
                                    extra_wait_pe=(tr_pe[t - 2] if t >= 2 else 0))
            y_done.append(ysc)
            tr_pe.append(pen)
            P.do("scalar", lambda e, t=t: e.add(
                ab[:, 2 * t:2 * t + 1], tiny[:, 3:4], 1.0), [(s_sc, 1)])
            P.w("vector", s_sc, s_sc.n)
            P.do("vector", lambda e, t=t: e.tensor_scalar(
                ab[:, 2 * t + 1:2 * t + 2], tiny[:, 1:2], tiny[:, 3:4], -1.0,
                op0=ALU.mult, op1=ALU.mult), [(s_dv, 1)])
        return s_dv.n, s_st.n

    def p3(p2_dv):
        wkv = A(o_wkv, wkv_sz)
        wvv = A(o_wkv + wkv_sz, wkv_sz)
        wqr = [A(o_wqr, KT * 256), A(o_wqr + KT * 256, KT * 256)]
        qsq = A(o_qsq, 2048)
        rsd = [A(o_rsd, 2048, F32), A(o_rsd + 2048, 2048, F32)]
        evk = [A(o_evk, 1024), A(o_evk + 1024, 1024)]

        cwT = min(512, TOK)

        def kvm_K(kv, ch):
            base = kv * 128 * TOK
            return kv_mine[base:base + 128 * TOK].rearrange(
                "(p chh f) -> chh p f", p=128, chh=NCH, f=cwT)[ch]

        def kvm_V(t):
            base = KVH * 128 * TOK + t * 128 * c.VW
            return kv_mine[base:base + 128 * c.VW].rearrange(
                "(p f) -> p f", p=128)
        P.do("gpsimd", lambda e: e.dma_start(
            out=wkv, in_=wk_d.rearrange("(kt p) i -> p kt i", p=128)), [(s_wg, 16)])
        P.do("gpsimd", lambda e: e.dma_start(
            out=wvv, in_=wv_d.rearrange("(kt p) i -> p kt i", p=128)), [(s_wg, 16)])
        kvw_need = s_wg.n

        def rms_chunk(ch, bank, gamma):
            P.w("scalar", s_dv, s_dv.n)
            P.w("scalar", s_pe, s_pe.n)
            P.do("scalar", lambda e: e.activation(qsq[:, 0:512], bank, AF.Square),
                 [(s_sc, 1)])
            msb = pB(1024, 512)
            P.w("tensor", s_sc, s_sc.n)
            P.do("tensor", lambda e, msb=msb: e.matmul(
                msb, ones128b[:], qsq[:, 0:512], start=True, stop=True),
                [(s_pe, 1)])
            P.w("scalar", s_pe, s_pe.n)
            P.do("scalar", lambda e, ch=ch, msb=msb: e.activation(
                rsd[ch % 2][:, 0:512], msb, AF.Ln, scale=1.0 / 128,
                bias=stat[:, 61:62]), [(s_sc, 1)])
            P.w("scalar", s_sc, s_sc.n)
            P.do("scalar", lambda e, ch=ch: e.activation(
                rsd[ch % 2][:, 0:512], rsd[ch % 2][:, 0:512], AF.Exp, scale=-0.5),
                [(s_sc, 1)])

        P.w("tensor", s_dv, p2_dv)
        P.w("tensor", s_wg, kvw_need)
        kev, kst = [], []
        for kv in range(KVH):
            for ch in range(NCH):
                bank = pA((len(kev) % 2) * 512, min(512, TOK))
                if len(kev) >= 2:
                    P.w("tensor", s_dv, kev[-2])
                cw = min(512, TOK)
                for k in range(KT):
                    P.do("tensor", lambda e, k=k, kv=kv, ch=ch, bank=bank, cw=cw:
                         e.matmul(bank,
                                  wkv[:, (k * KVH + kv) * 128:(k * KVH + kv + 1) * 128],
                                  A(o_y2T + k * TOKB, TOKB)[:, ch * 512:ch * 512 + cw],
                                  start=(k == 0), stop=(k == KT - 1)),
                         [(s_pe, 1)] if k == KT - 1 else ())
                rms_chunk(ch, bank, True)
                slot = len(kst) % 2
                P.w("vector", s_sc, s_sc.n)
                if len(kst) >= 2:
                    P.w("vector", s_st, kst[-2])
                P.do("vector", lambda e, ch=ch, bank=bank, slot=slot:
                     e.scalar_tensor_tensor(evk[slot][:, 0:512], bank,
                                            gpcol[:, 0:1], rsd[ch % 2][:, 0:512],
                                            op0=ALU.mult, op1=ALU.mult),
                     [(s_dv, 1)])
                kev.append(s_dv.n)
                P.w("sync", s_dv, s_dv.n)
                P.do("sync", lambda e, kv=kv, ch=ch, slot=slot, cw=cw: e.dma_start(
                    out=kvm_K(kv, ch), in_=evk[slot][:, 0:cw]), [(s_st, 16)])
                kst.append(s_st.n)
        vev = []
        voff = KVH * 128 * TOK
        for t in range(NT):
            bank = pA(1024 + (t % 2) * 512, c.VW)
            if len(vev) >= 2:
                P.w("tensor", s_dv, vev[-2])
            for k in range(KT):
                P.do("tensor", lambda e, k=k, t=t, bank=bank: e.matmul(
                    bank, A(o_y2T + k * TOKB, TOKB)[:, t * 128:(t + 1) * 128],
                    wvv[:, k * c.VW:(k + 1) * c.VW],
                    start=(k == 0), stop=(k == KT - 1)),
                    [(s_pe, 1)] if k == KT - 1 else ())
            P.w("vector", s_pe, s_pe.n)
            slot = len(kst) % 2
            if len(kst) >= 2:
                P.w("vector", s_st, kst[-2])
            P.do("vector", lambda e, bank=bank, slot=slot: e.tensor_copy(
                evk[slot][:, 0:c.VW], bank), [(s_dv, 1)])
            vev.append(s_dv.n)
            P.w("sync", s_dv, s_dv.n)
            P.do("sync", lambda e, t=t, slot=slot: e.dma_start(
                out=kvm_V(t), in_=evk[slot][:, 0:c.VW]), [(s_st, 16)])
            kst.append(s_st.n)
        P.w("gpsimd", s_st, kst[-1])
        P.do("gpsimd", lambda e: e.dma_start(
            out=sh_kv[vh_cell["v"]], in_=kv_mine), [(s_kvw, 16)])
        P.w("gpsimd", s_kvw, s_kvw.n)
        fl3 = sh_flag.rearrange("h (o f) -> h o f", o=1)
        P.do("gpsimd", lambda e: e.dma_start(
            out=fl3[vh_cell["v"]], in_=ivec[0:1, 64:128]),
            [(s_kvw, 16)])

        q_need, qh_done, qev = {}, [], []

        def emit_wq_load(hd):
            if hd >= 2:
                P.w("gpsimd", s_pe, qh_done[hd - 2])
            P.do("gpsimd", lambda e, hd=hd: e.dma_start(
                out=wqr[hd % 2][:],
                in_=wq_d[:, hd * 128:(hd + 1) * 128].rearrange(
                    "(kt p) i -> p kt i", p=128)), [(s_wg, 16)])
            q_need[hd] = s_wg.n

        for hd in range(min(2, H)):
            emit_wq_load(hd)
        for hd in range(H):
            P.w("tensor", s_wg, q_need[min(hd + 1, 1) if hd < 2 else hd]
                if hd < 2 else q_need[hd])
            for ch in range(NCH):
                cw = min(512, TOK)
                bank = pA((len(qev) % 2) * 512, cw)
                if len(qev) >= 2:
                    P.w("tensor", s_dv, qev[-2])
                for k in range(KT):
                    P.do("tensor", lambda e, k=k, hd=hd, ch=ch, bank=bank, cw=cw:
                         e.matmul(bank, wqr[hd % 2][:, k * 128:(k + 1) * 128],
                                  A(o_y2T + k * TOKB, TOKB)[:,
                                    ch * 512:ch * 512 + cw],
                                  start=(k == 0), stop=(k == KT - 1)),
                         [(s_pe, 1)] if k == KT - 1 else ())
                rms_chunk(ch, bank, False)
                P.w("vector", s_sc, s_sc.n)
                P.do("vector", lambda e, hd=hd, ch=ch, bank=bank, cw=cw:
                     e.tensor_tensor(
                         A(o_QT + hd * TOKB, TOKB)[:, ch * 512:ch * 512 + cw],
                         bank, rsd[ch % 2][:, 0:cw], op=ALU.mult), [(s_dv, 1)])
                qev.append(s_dv.n)
            qh_done.append(s_pe.n)
            if hd + 2 < H:
                emit_wq_load(hd + 2)
        qt_dv = s_dv.n

        def barrier(e):
            with (e.register("r_k") as r_k, e.register("r_f0") as r_f0,
                  e.register("r_f1") as r_f1, e.register("r_nb") as r_nb):
                e.reg_load(r_nb, ivec[0:1, 64:65])
                e.reg_mov(r_k, 0)
                lbl, lbl_end = "kvpoll", "kvpoll_end"
                nc.br(lbl, engines=[e.engine])
                with nc.body(lbl, valid_engines=[e.engine]):
                    e.dma_start(out=fpoll[:],
                                in_=sh_flag.rearrange("a b -> (a b)").unsqueeze(0)
                                ).then_inc(s_bar.h, 16)
                    e.reg_add(r_k, r_k, 16)
                    e.wait_ge(s_bar.h, r_k)
                    e.reg_load(r_f0, fpoll[0:1, 0:1])
                    e.reg_load(r_f1, fpoll[0:1, 64:65])
                    e.reg_alu(r_f0, r_f0, r_nb, op=ALU.bitwise_xor)
                    e.reg_alu(r_f1, r_f1, r_nb, op=ALU.bitwise_xor)
                    e.reg_alu(r_f0, r_f0, r_f1, op=ALU.bitwise_or)
                    nc.br_ne(r_f0, 0, on_true=lbl, on_false=lbl_end,
                             engines=[e.engine])
                nc.switch_bb(lbl_end)
            return e.nop()
        P.raw("gpsimd", barrier)
        P.do("gpsimd", lambda e: e.nop(), [(s_gb, 1)])
        gb_bar = s_gb.n
        P.w("sync", s_gb, gb_bar)
        kv_ld = [0] * KVH
        def emit_kf(kv):
            for hh in range(2):
                P.do("sync", lambda e, kv=kv, hh=hh: e.dma_start(
                    out=A(o_Kf + kv * c.S * 2, c.S * 2).rearrange(
                        "p (lt two f) -> p lt two f", two=2, f=128)[:, :, hh, :],
                    in_=sh_kv[hh][kv * 128 * TOK:(kv + 1) * 128 * TOK].rearrange(
                        "(p lt f) -> p lt f", p=128, f=128)), [(s_x, 16)])
            kv_ld[kv] = s_x.n
        emit_kf(0)
        for hh in range(2):
            P.do("sync", lambda e, hh=hh: e.dma_start(
                out=A(o_Vf, c.SKT * c.VW * 2).rearrange(
                    "p (lt two f) -> p lt two f", two=2, f=c.VW)[:, :, hh, :],
                in_=sh_kv[hh][voff:voff + NT * 128 * c.VW].rearrange(
                    "(t p f) -> p t f", p=128, f=c.VW)), [(s_x, 16)])
        vf_ld = s_x.n
        for kv in range(1, KVH):
            emit_kf(kv)
        kvf_ld = (kv_ld, vf_ld)
        if dbg:
            P.w("sync", s_kvw, s_kvw.n)
            P.w("sync", s_gb, gb_bar)
            P.do("sync", lambda e: e.dma_start(out=dbg_t["dkv"], in_=sh_kv),
                 [(s_st, 16)])
        return qt_dv, kvf_ld

    def p4(qt_dv, kvf_ld):
        Kf = lambda kv: A(o_Kf + kv * c.S * 2, c.S * 2)
        Vf = A(o_Vf, c.SKT * c.VW * 2)
        pT = [A(o_pT, 4096), A(o_pT + 4096, 4096)]
        m1b = [A(o_mp1 + i * 1024, 1024) for i in range(4)]
        dump = A(o_dump, 4096)
        atev = [A(o_atev, 1024), A(o_atev + 1024, 1024)]
        kv_ld, vf_ld = kvf_ld
        P.w("tensor", s_dv, qt_dv)
        exp_done, at_st, av_mm = [], [], []
        last_exp_sc = 0
        last_psA_dv = 0
        last_at_ev = [0]
        rr_cp = []
        rr_dma = []
        frep_sc = []
        bq_dv = {}

        def gen_pass1(a, hd, mld):
            nonlocal last_psA_dv
            kvv = hd // (H // KVH)
            nchs = [(4 * a + s_ + 2) // 2 for s_ in range(4)]
            cum = [sum(nchs[:s_]) for s_ in range(4)]
            sstf = lambda s_: A(o_sst + cum[s_] * 1024, nchs[s_] * 1024)
            P.w("tensor", s_dv, last_psA_dv)
            P.w("tensor", s_x, kv_ld[kvv])
            if rr_cp:
                P.w("tensor", s_dv, rr_cp[-1])
            ring = []
            for s_ in range(4):
                nch = nchs[s_]
                for cc in range(nch):
                    bk = len(ring) % 3
                    if len(ring) >= 3:
                        dvt, sct = ring[-3]
                        P.w("tensor", s_dv, dvt)
                        P.w("tensor", s_sc, sct)
                    P.do("tensor", lambda e, hd=hd, s_=s_, a=a, kvv=kvv, cc=cc,
                         bk=bk: e.matmul(
                             pA(bk * 512, 512),
                             A(o_QT + hd * TOKB, TOKB)[:,
                               (4 * a + s_) * 128:(4 * a + s_ + 1) * 128],
                             Kf(kvv)[:, cc * 512:(cc + 1) * 512],
                             start=True, stop=True), [(s_pe, 1)])
                    mm1 = s_pe.n
                    P.w("vector", s_pe, mm1)
                    if cc == nch - 1:
                        P.w("vector", s_x, mld)
                        P.do("vector", lambda e, s_=s_, bk=bk: e.tensor_tensor(
                            pA(bk * 512, 512), pA(bk * 512, 512), m1b[s_][:],
                            op=ALU.add), [(s_dv, 1)])
                        P.w("vector", s_dv, s_dv.n)
                    P.do("vector", lambda e, s_=s_, cc=cc, bk=bk: e.reduce_max(
                        stat[:, 8 + 8 * s_ + cc:9 + 8 * s_ + cc],
                        pA(bk * 512, 512), axis=AX.X, negate=True), [(s_dv, 1)])
                    P.w("scalar", s_dv, s_dv.n)
                    P.do("scalar", lambda e, s_=s_, cc=cc, bk=bk: e.copy(
                        sstf(s_)[:, cc * 512:(cc + 1) * 512],
                        pA(bk * 512, 512)), [(s_sc, 1)])
                    ring.append((s_dv.n, s_sc.n))
                    if cc % 3 == 2 or cc == nch - 1:
                        yield
            last_psA_dv = s_dv.n
            phaseA_sc = s_sc.n
            P.w("vector", s_dv, s_dv.n)
            for s_ in range(4):
                if nchs[s_] > 1:
                    P.do("vector", lambda e, s_=s_, n=nchs[s_]: e.tensor_reduce(
                        stat[:, 40 + s_:41 + s_],
                        stat[:, 8 + 8 * s_:8 + 8 * s_ + n], axis=AX.X,
                        op=ALU.min), [(s_dv, 1)])
                else:
                    P.do("vector", lambda e, s_=s_: e.tensor_copy(
                        stat[:, 40 + s_:41 + s_], stat[:, 8 + 8 * s_:9 + 8 * s_]),
                        [(s_dv, 1)])
            P.w("vector", s_dv, s_dv.n)
            for s_ in range(4):
                P.do("vector", lambda e, s_=s_: e.tensor_scalar(
                    stat[:, 44 + s_:45 + s_], stat[:, 40 + s_:41 + s_], c.SC,
                    None, op0=ALU.mult), [(s_dv, 1)])
            nm_dv = s_dv.n
            P.w("scalar", s_dv, nm_dv)
            P.w("scalar", s_sc, phaseA_sc)
            exp_sc = [0, 0, 0, 0]
            l2 = []
            for s_ in range(4):
                nch = nchs[s_]
                parts = [(0, min(4, nch))] + ([(4, nch - 4)] if nch > 4 else [])
                for pi, (pc0, pn) in enumerate(parts):
                    P.do("scalar", lambda e, s_=s_, pc0=pc0, pn=pn, pi=pi:
                         e.activation(
                             dump[:, 0:pn * 512],
                             sstf(s_)[:, pc0 * 512:(pc0 + pn) * 512], AF.Exp,
                             scale=c.SC, bias=stat[:, 44 + s_:45 + s_],
                             accum_out=stat[:, (48 if pi == 0 else 56) +
                                            s_:(49 if pi == 0 else 57) + s_]),
                         [(s_sc, 1)])
                exp_sc[s_] = s_sc.n
                if len(parts) > 1:
                    l2.append(s_)
            yield
            if l2:
                P.w("vector", s_sc, exp_sc[l2[-1]])
                for s_ in l2:
                    P.do("vector", lambda e, s_=s_: e.tensor_tensor(
                        stat[:, 48 + s_:49 + s_], stat[:, 48 + s_:49 + s_],
                        stat[:, 56 + s_:57 + s_], op=ALU.add), [(s_dv, 1)])
                P.w("scalar", s_dv, s_dv.n)
            for s_ in range(4):
                P.w("scalar", s_sc, exp_sc[s_])
                P.do("scalar", lambda e, s_=s_: e.activation(
                    stat[:, 52 + s_:53 + s_], stat[:, 48 + s_:49 + s_], AF.Ln),
                    [(s_sc, 1)])
            ln_sc = s_sc.n
            P.w("vector", s_sc, ln_sc)
            for s_ in range(4):
                P.do("vector", lambda e, s_=s_: e.tensor_scalar(
                    stat[:, 52 + s_:53 + s_], stat[:, 52 + s_:53 + s_],
                    1.0 / c.SC, None, op0=ALU.mult), [(s_dv, 1)])
            P.w("vector", s_dv, s_dv.n)
            for s_ in range(4):
                P.do("vector", lambda e, s_=s_: e.tensor_tensor(
                    stat[:, 4 + s_:5 + s_], stat[:, 40 + s_:41 + s_],
                    stat[:, 52 + s_:53 + s_], op=ALU.subtract), [(s_dv, 1)])
            r_dv = s_dv.n
            yield
            P.w("tensor", s_dv, r_dv)
            P.do("tensor", lambda e: e.transpose(
                pA(1024, 128)[0:4, :], stat[:, 4:8], ident32[:]), [(s_pe, 1)])
            P.w("vector", s_pe, s_pe.n)
            if rr_dma:
                P.w("vector", s_rr, rr_dma[-1])
            P.do("vector", lambda e: e.tensor_copy(
                rr4[:], pA(1024, 128)[0:4, :]), [(s_dv, 1)])
            rr_cp.append(s_dv.n)
            P.w("scalar", s_dv, s_dv.n)
            P.do("scalar", lambda e: e.dma_start(out=rrow[0:1, :], in_=rr4[:]),
                 [(s_rr, 16)])
            rr_dma.append(s_rr.n)
            sl = hd % 2
            P.w("vector", s_rr, rr_dma[-1])
            P.w("vector", s_pe, s_pe.n)
            P.do("vector", lambda e, sl=sl: e.tensor_copy(rhis[sl][:], rrow[:]),
                 [(s_dv, 1)])
            P.w("vector", s_dv, s_dv.n)
            P.do("vector", lambda e, sl=sl: e.tensor_tensor(
                rlos[sl][:], rrow[:], rhis[sl][:], op=ALU.subtract), [(s_dv, 1)])
            bq_dv[hd] = s_dv.n

        def gen_pass2(a, hd):
            kvv = hd // (H // KVH)
            sl = hd % 2
            kt_max = min(8 * a + 8, c.SKT)
            P.w("tensor", s_dv, bq_dv[hd])
            P.w("tensor", s_dv, last_at_ev[0])
            P.w("tensor", s_x, kv_ld[kvv])
            P.w("tensor", s_x, vf_ld)
            if frep_sc:
                P.w("tensor", s_sc, frep_sc[-1])
            if exp_done:
                P.w("tensor", s_sc, exp_done[-1])
            P.do("tensor", lambda e, sl=sl: e.matmul(
                pB(1024, 512), onesrowb[:], rlos[sl][:], start=True,
                stop=True), [(s_pe, 1)])
            P.w("scalar", s_pe, s_pe.n)
            P.do("scalar", lambda e: e.activation(
                A(o_frep, 2048, F32), pB(1024, 512), AF.Exp, scale=c.SC),
                [(s_sc, 1)])
            frep_sc.append(s_sc.n)
            t = 0
            pair = 0
            while t < kt_max:
                npr = min(2, kt_max - t)
                b0 = (pair % 2) * 1024
                if len(exp_done) >= 2:
                    P.w("tensor", s_sc, exp_done[-2])
                if b0 == 1024 and frep_sc:
                    P.w("tensor", s_sc, frep_sc[-1])
                for i in range(npr):
                    bank = pB(b0 + i * 512, 512)
                    P.do("tensor", lambda e, hd=hd, kvv=kvv, t=t, i=i, a=a,
                         bank=bank: e.matmul(
                             bank, Kf(kvv)[:, (t + i) * 128:(t + i + 1) * 128],
                             A(o_QT + hd * TOKB, TOKB)[:,
                               a * 512:(a + 1) * 512],
                             start=True, stop=False), ())
                    P.do("tensor", lambda e, bank=bank, sl=sl: e.matmul(
                        bank, onesrowb[:], rhis[sl][:], start=False,
                        stop=True), [(s_pe, 1)] if i == npr - 1 else ())
                mm2 = s_pe.n
                need_dv = False
                for i in range(npr):
                    d = t + i - 8 * a
                    if 0 <= d < 8:
                        P.w("vector", s_pe, mm2)
                        P.do("vector", lambda e, d=d, b=pB(b0 + i * 512, 512):
                             e.tensor_tensor(b, b, A(o_mp2 + d * 1024, 1024),
                                             op=ALU.add), [(s_dv, 1)])
                        need_dv = True
                if need_dv:
                    P.w("scalar", s_dv, s_dv.n)
                else:
                    P.w("scalar", s_pe, mm2)
                slot = pair % 2
                if len(av_mm) >= 2:
                    P.w("scalar", s_pe, av_mm[-2])
                P.do("scalar", lambda e, slot=slot, b0=b0, npr=npr: e.activation(
                    pT[slot][:, 0:npr * 512], pB(b0, npr * 512), AF.Exp,
                    scale=c.SC), [(s_sc, 1)])
                exp_done.append(s_sc.n)
                P.w("tensor", s_sc, s_sc.n)
                for i in range(npr):
                    P.do("tensor", lambda e, t=t, i=i, kvv=kvv, slot=slot,
                         kt_max=kt_max: e.matmul(
                             pA(1536, 512),
                             Vf[:, (t + i) * c.VW + kvv * 128:
                                (t + i) * c.VW + (kvv + 1) * 128],
                             pT[slot][:, i * 512:(i + 1) * 512],
                             start=(t + i == 0), stop=(t + i == kt_max - 1)),
                         [(s_pe, 1)] if i == npr - 1 else ())
                av_mm.append(s_pe.n)
                t += npr
                pair += 1
                yield
            P.w("vector", s_pe, av_mm[-1])
            P.w("vector", s_sc, frep_sc[-1])
            sl2 = (a * H + hd) % 2
            if len(at_st) >= 2:
                P.w("vector", s_st, at_st[-2])
            P.do("vector", lambda e, sl2=sl2: e.tensor_tensor(
                atev[sl2][:, 0:512], pA(1536, 512), A(o_frep, 2048, F32),
                op=ALU.mult), [(s_dv, 1)])
            last_at_ev[0] = s_dv.n
            P.w("sync", s_dv, s_dv.n)
            P.do("sync", lambda e, hd=hd, a=a, sl2=sl2: e.dma_start(
                out=attnTd[hd][:, a * 512:(a + 1) * 512],
                in_=atev[sl2][:, 0:512]), [(s_st, 16)])
            at_st.append(s_st.n)

        for a in range(c.NG):
            P.w("sync", s_dv, s_dv.n)
            P.w("sync", s_sc, s_sc.n)
            for d in range(8):
                P.do("sync", lambda e, a=a, d=d: e.dma_start(
                    out=A(o_mp2 + d * 1024, 1024), in_=mp2_d[a, d]), [(s_x, 16)])
            for s_ in range(4):
                P.do("sync", lambda e, a=a, s_=s_: e.dma_start(
                    out=m1b[s_][:], in_=mp1_d[4 * a + s_]), [(s_x, 16)])
            mld = s_x.n
            for hd in range(H):
                g1 = gen_pass1(a, hd, mld)
                g2 = gen_pass2(a, hd - 1) if hd > 0 else None
                alive = True
                while alive:
                    alive = False
                    try:
                        next(g1)
                        alive = True
                    except StopIteration:
                        pass
                    if g2 is not None:
                        try:
                            next(g2)
                            alive = True
                        except StopIteration:
                            pass
            for _ in gen_pass2(a, H - 1):
                pass
        if dbg:
            P.w("sync", s_st, s_st.n)
            P.do("sync", lambda e: e.dma_start(out=dbg_t["dattnT"], in_=attnTd),
                 [(s_st, 16)])
        return s_st.n

    def p5(at_total, x1_st):
        P.w("sync", s_st, at_total)
        for hd in range(H):
            P.do("sync", lambda e, hd=hd: e.dma_start(
                out=A(o_aT + hd * TOKB, TOKB), in_=attnTd[hd]), [(s_x, 16)])
        aT_ld = s_x.n
        for hk in range(H):
            P.do("gpsimd", lambda e, hk=hk: e.dma_start(
                out=A(o_wo + hk * DB, DB), in_=wo_d[hk * 128:(hk + 1) * 128, :]),
                [(s_wg, 16)])
        wo_ld = s_wg.n
        x1b = [A(o_x1b, D * 4, F32), A(o_x1b + D * 4, D * 4, F32)]
        u1 = A(o_u1, D * 4, F32)
        x2b = [A(o_x2b, D * 4, F32), A(o_x2b + D * 4, D * 4, F32)]
        P.w("tensor", s_x, aT_ld)
        P.w("tensor", s_wg, wo_ld)
        x2_dv, x2_st = [], []
        ld = []

        def p5_load(t):
            P.w("sync", s_st, x1_st)
            if t >= 2:
                P.w("sync", s_dv, x2_dv[t - 2])
            P.do("sync", lambda e, t=t: e.dma_start(
                out=x1b[t % 2][:], in_=x1d[t * 128:(t + 1) * 128, :]), [(s_x, 16)])
            ld.append(s_x.n)

        p5_load(0)
        for t in range(NT):
            if t + 1 < NT:
                p5_load(t + 1)
            x1_ld = ld[-1]
            ps = psA if t % 2 == 0 else psB
            if t >= 2:
                P.w("tensor", s_dv, x2_dv[t - 2])
            for hk in range(H):
                for m in range(NM):
                    P.do("tensor", lambda e, t=t, hk=hk, m=m, ps=ps: e.matmul(
                        ps[:, m * 512:(m + 1) * 512],
                        A(o_aT + hk * TOKB, TOKB)[:, t * 128:(t + 1) * 128],
                        A(o_wo + hk * DB, DB)[:, m * 512:(m + 1) * 512],
                        start=(hk == 0), stop=(hk == H - 1)),
                        [(s_pe, 1)] if (hk == H - 1 and m == NM - 1) else ())
            P.w("vector", s_pe, s_pe.n)
            P.w("vector", s_x, x1_ld)
            if t >= 2:
                P.w("vector", s_st, x2_st[t - 2])
            P.do("vector", lambda e, t=t: e.tensor_scalar(
                u1[:], x1b[t % 2][:], ab[:, 2 * t:2 * t + 1],
                ab[:, 2 * t + 1:2 * t + 2], op0=ALU.mult, op1=ALU.add),
                [(s_dv, 1)])
            P.w("vector", s_dv, s_dv.n)
            P.do("vector", lambda e, t=t, ps=ps: e.tensor_tensor(
                x2b[t % 2][:], u1[:], ps[:, 0:D], op=ALU.add), [(s_dv, 1)])
            x2_dv.append(s_dv.n)
            P.w("sync", s_dv, s_dv.n)
            P.do("sync", lambda e, t=t: e.dma_start(
                out=x2d[t * 128:(t + 1) * 128, :], in_=x2b[t % 2][:]), [(s_st, 16)])
            x2_st.append(s_st.n)
        if dbg:
            P.do("sync", lambda e: e.dma_start(out=dbg_t["dx2"], in_=x2d),
                 [(s_st, 16)])
        return s_st.n

    def p7(acc_total, x2_st_total):
        b1 = [A(o_7a, D * 4, F32), A(o_7a + D * 4, D * 4, F32)]
        b2 = [A(o_7b, D * 4, F32), A(o_7b + D * 4, D * 4, F32)]
        b3 = [A(o_7c, D * 4, F32), A(o_7c + D * 4, D * 4, F32)]
        P.w("sync", s_acc, acc_total)
        P.w("sync", s_st, x2_st_total)
        o_dv, o_st = [], []
        ld = []

        def p7_load(t):
            if t >= 2:
                P.w("sync", s_dv, o_dv[t - 2])
                P.w("sync", s_st, o_st[t - 2])
            P.do("sync", lambda e, t=t: e.dma_start(
                out=b1[t % 2][:], in_=x2d[t * 128:(t + 1) * 128, :]), [(s_x, 16)])
            P.do("sync", lambda e, t=t: e.dma_start(
                out=b2[t % 2][:], in_=accd[t * 128:(t + 1) * 128, :]), [(s_x, 16)])
            ld.append(s_x.n)

        p7_load(0)
        for t in range(NT):
            if t + 1 < NT:
                p7_load(t + 1)
            P.w("vector", s_x, ld[-1])
            P.do("vector", lambda e, t=t: e.tensor_tensor(
                b3[t % 2][:], b1[t % 2][:], b2[t % 2][:], op=ALU.add), [(s_dv, 1)])
            o_dv.append(s_dv.n)
            P.w("sync", s_dv, s_dv.n)
            P.do("sync", lambda e, t=t: e.dma_start(
                out=out_d[t * 128:(t + 1) * 128, :], in_=b3[t % 2][:]),
                [(s_st, 16)])
            o_st.append(s_st.n)
        return s_st.n

    p0()
    fence()
    f0_dv, f0_sc, _ = ffn_f0(x0, 0)
    acc1 = ffn_f1(f0_dv, f0_sc)
    if dbg:
        P.w("sync", s_acc, acc1)
        P.do("sync", lambda e: e.dma_start(out=dbg_t["dacc1"], in_=accd),
             [(s_st, 16)])
    dacc1_st = s_st.n
    fence()
    p2_dv, x1_st = p2(acc1)
    if dbg:
        P.w("sync", s_st, x1_st)
        P.do("sync", lambda e: e.dma_start(out=dbg_t["dx1"], in_=x1d),
             [(s_st, 16)])
    fence()
    qt_dv, kvf_ld = p3(p2_dv)
    fence()
    at_total = p4(qt_dv, kvf_ld)
    fence()
    x2_st = p5(at_total, x1_st)
    fence()
    f0_dv2, f0_sc2, _ = ffn_f0(x2d, x2_st)
    if dbg:
        P.w("gpsimd", s_st, dacc1_st)
    acc2 = ffn_f1(f0_dv2, f0_sc2)
    fence()
    final = p7(acc2, x2_st)
    P.w("scalar", s_st, final)
    P.w("scalar", s_acc, s_acc.n)

    with nc.Block() as block:
        block.sync(lambda e: P.replay("sync", e))
        block.gpsimd(lambda e: P.replay("gpsimd", e))
        block.tensor(lambda e: P.replay("tensor", e))
        block.vector(lambda e: P.replay("vector", e))
        block.scalar(lambda e: P.replay("scalar", e))

    stack.close()
    return nc


def make_masks(cfg, h):
    c = cfg
    NT = c.NT
    q1 = np.arange(128)
    mp1 = np.zeros((NT, 128, 512), np.float32)
    for j in range(NT):
        g = 2 * j + h
        a_last = ((g + 1) * 128 + 511) // 512 - 1
        kk = a_last * 512 + np.arange(512)
        qq = g * 128 + q1
        mp1[j] = np.where(kk[None, :] <= qq[:, None], 0.0, NEG)
    mp2 = np.zeros((c.NG, 8, 128, 512), np.float32)
    for a in range(c.NG):
        for d in range(8):
            t = 8 * a + d
            kk = t * 128 + q1
            qi = np.concatenate([(8 * a + 2 * s + h) * 128 + q1 for s in range(4)])
            if t >= c.SKT:
                mp2[a, d] = 0.0
            else:
                mp2[a, d] = np.where(kk[:, None] <= qi[None, :], 0.0, NEG)
    return mp1.astype(ml_dtypes.bfloat16), mp2.astype(ml_dtypes.bfloat16)


def make_in_maps(cfg, inputs, nonce):
    c = cfg
    x = np.ascontiguousarray(inputs["L0_attn_x"])
    wq = np.ascontiguousarray(inputs["L0_attn_Wq"])
    wk = np.ascontiguousarray(inputs["L0_attn_Wk"])
    wv = np.ascontiguousarray(inputs["L0_attn_Wv"])
    wo = np.ascontiguousarray(inputs["L0_attn_Wo"])
    gq = np.ascontiguousarray(inputs["L0_attn_q_gamma"]).reshape(1, 128)
    gk = np.ascontiguousarray(inputs["L0_attn_k_gamma"]).reshape(1, 128)
    r9w = np.ascontiguousarray(np.concatenate(
        [inputs["L0_ffn_router_w"].T,
         inputs["L0_ffn_shared_gate_w"].reshape(-1, 1)], axis=1))
    wgate = np.ascontiguousarray(inputs["L0_ffn_w_gate"])
    wup = np.ascontiguousarray(inputs["L0_ffn_w_up"])
    wdown = np.ascontiguousarray(inputs["L0_ffn_w_down"])
    swg = np.ascontiguousarray(inputs["L0_ffn_sw_gate"])
    swu = np.ascontiguousarray(inputs["L0_ffn_sw_up"])
    swd = np.ascontiguousarray(inputs["L0_ffn_sw_down"])
    ident = np.eye(128, dtype=np.float32)
    identb = ident.astype(ml_dtypes.bfloat16)
    in_maps = []
    for core in range(8):
        b, h = core // 2, core % 2
        xs = np.ascontiguousarray(
            x[b].reshape(2 * c.NT, 128, c.D)[h::2].reshape(c.TOK, c.D))
        mp1, mp2 = make_masks(cfg, h)
        in_maps.append({
            "x0": xs, "wq": wq, "wk": wk, "wv": wv, "wo": wo,
            "gq": gq, "gk": gk, "r9w": r9w,
            "wgate": wgate, "wup": wup, "wdown": wdown,
            "swg": swg, "swu": swu, "swd": swd,
            "mp1": mp1, "mp2": mp2,
            "hsel": np.full((1, 64), h, dtype=np.int32),
            "noncein": np.full((1, 64), nonce, dtype=np.int32),
            "identb": identb, "ident32": ident,
        })
    return in_maps


def kernel(**inputs):
    cfg = FULL
    nonce = np.int32(np.random.default_rng().integers(1, 2**31 - 1))
    in_maps = make_in_maps(cfg, inputs, nonce)
    nc = build(cfg)
    res = run_bass_kernel_spmd(nc, in_maps, core_ids=list(range(8)))
    out = np.empty((4, 2 * cfg.TOK, cfg.D), np.float32)
    for core in range(8):
        b, h = core // 2, core % 2
        o = res.results[core]["out"].reshape(cfg.NT, 128, cfg.D)
        out[b].reshape(2 * cfg.NT, 128, cfg.D)[h::2] = o
    return out
